# version 05
# baseline (speedup 1.0000x reference)
import contextlib
import numpy as np
import ml_dtypes
import concourse.bass as bass
import concourse.mybir as mybir
from concourse.bass_utils import run_bass_kernel_spmd

F32 = mybir.dt.float32
BF16 = mybir.dt.bfloat16
AF = mybir.ActivationFunctionType
ALU = mybir.AluOpType
AX = mybir.AxisListType

NCORE = 8
TPC = 1024
NT = 8
D = 2048
ND = 16
H = 16
NCT = 48
NE = 9
NF = 8
ISQ = float(1.0 / np.sqrt(128.0))
EPS_LN = 1e-5
EPS_RMS = 1e-6
PAIRS = [[0, 1], [2, 3], [4, 5], [6, 7]]


def build_program(dbg=(), upto=99):
    dbg = set(dbg)
    nc = bass.Bass("TRN2", target_bir_lowering=False, debug=False)

    x_my = nc.dram_tensor("x_my", [TPC, D], F32, kind="ExternalInput").ap()
    wqkv = nc.dram_tensor("wqkv", [NCT, 128, ND, 128], F32, kind="ExternalInput").ap()
    wo = nc.dram_tensor("wo", [ND, 128, ND, 128], F32, kind="ExternalInput").ap()
    router = nc.dram_tensor("router", [128, ND, 8], F32, kind="ExternalInput").ap()
    rmst_in = nc.dram_tensor("rmst_in", [128, ND], F32, kind="ExternalInput").ap()
    rmsrow_in = nc.dram_tensor("rmsrow_in", [1, D], F32, kind="ExternalInput").ap()
    wg_in = nc.dram_tensor("wg_in", [NE, NF, 128, ND, 128], F32, kind="ExternalInput").ap()
    wu_in = nc.dram_tensor("wu_in", [NE, NF, 128, ND, 128], F32, kind="ExternalInput").ap()
    wd_in = nc.dram_tensor("wd_in", [NE, ND, 128, NF, 128], F32, kind="ExternalInput").ap()
    hsel = nc.dram_tensor("hsel", [1, 2], F32, kind="ExternalInput").ap()
    o_my = nc.dram_tensor("o_my", [TPC, D], F32, kind="ExternalOutput").ap()

    kin = nc.dram_tensor("kin", [H * 128, TPC], BF16, kind="Internal").ap()
    kag = nc.dram_tensor("kag", [2 * H * 128, TPC], BF16, kind="Internal").ap()
    cin = nc.dram_tensor("cin", [H, 2048], F32, kind="Internal").ap()
    car = nc.dram_tensor("car", [H, 2048], F32, kind="Internal").ap()

    dbg_out = {}
    def mkdbg(name, shape, dt):
        if name in dbg:
            dbg_out[name] = nc.dram_tensor("dbg_" + name, shape, dt, kind="ExternalOutput").ap()
    mkdbg("ln1T", [128, ND * TPC], BF16)
    mkdbg("qT", [128, H * TPC], BF16)
    mkdbg("vT", [128, H * TPC], BF16)
    mkdbg("kag", [2 * H * 128, TPC], BF16)
    mkdbg("cin", [H, 2048], F32)
    mkdbg("car", [H, 2048], F32)
    mkdbg("values", [128, H * TPC], BF16)
    mkdbg("x1T", [128, ND * TPC], F32)
    mkdbg("x1b", [128, ND * TPC], BF16)
    mkdbg("stats", [1, 4096], F32)
    mkdbg("h2T", [128, ND * TPC], BF16)
    mkdbg("gates", [128, 64], BF16)
    mkdbg("grows", [1, 8192], BF16)

    es = contextlib.ExitStack()
    SB = lambda n, s, d: es.enter_context(nc.sbuf_tensor(n, s, d))
    PS = lambda n, s, d: es.enter_context(nc.psum_tensor(n, s, d))
    SEM = lambda n: es.enter_context(nc.semaphore(n))

    AR1 = SB("AR1", [128, 16384], BF16)
    AR2 = SB("AR2", [128, 16384], BF16)
    AR3 = SB("AR3", [128, 16384], BF16)
    AR4 = SB("AR4", [128, 16384], BF16)
    kf_r = SB("kf_r", [128, 2 * 2048], BF16)
    kt_r = SB("kt_r", [128, 2 * 1024], BF16)
    S_r = SB("S_r", [128, 3 * 2048], BF16)
    wd_r = SB("wd_r", [128, 4 * 1024], BF16)
    xc_r = SB("xc_r", [128, 2 * 1024], F32)
    xrT = SB("xrT", [128, 1024], F32)
    rows = SB("rows", [1, 4096], F32)
    sigb = SB("sigb", [1, 1024], BF16)
    nmsb = SB("nmsb", [1, 1024], BF16)
    rmsrow_b = SB("rmsrow_b", [1, D], BF16)
    smallf = SB("smallf", [128, 48], F32)
    rbuf = SB("rbuf", [128, 16], BF16)
    logit_sb = SB("logit_sb", [128, 8], F32)
    gcolb = SB("gcolb", [128, 64], BF16)
    gscr = SB("gscr", [128, 16], F32)
    ident_f = SB("ident_f", [128, 128], F32)
    ident_b = SB("ident_b", [128, 128], BF16)
    ones_r = SB("ones_r", [1, 128], BF16)
    ones_c = SB("ones_c", [128, 1], BF16)
    rmst = SB("rmst", [128, ND], F32)
    router_b = SB("router_b", [128, ND * 8], BF16)
    hs_sb = SB("hs_sb", [1, 2], F32)
    gbc_sb = SB("gbc_sb", [128, 1024], BF16)
    smb_sb = SB("smb_sb", [128, 1], F32)

    AR1f = AR1[:, :].bitcast(F32)
    AR2f = AR2[:, :].bitcast(F32)
    AR3f = AR3[:, :].bitcast(F32)
    AR4f = AR4[:, :].bitcast(F32)
    growsb = rows[:, :].bitcast(BF16)
    chb = rows[0:1, 3072:4096].bitcast(BF16)

    def x1T(ot, lo=0, n=1024):
        if ot < 8:
            return AR2f[:, ot * 1024 + lo: ot * 1024 + lo + n]
        return AR3f[:, (ot - 8) * 1024 + lo: (ot - 8) * 1024 + lo + n]

    Q0 = PS("Q0", [128, 1024], F32)
    Q1 = PS("Q1", [128, 1024], F32)
    Q2 = PS("Q2", [128, 1024], F32)
    Q3 = PS("Q3", [128, 1024], F32)
    QQ = [Q0, Q1, Q2, Q3]

    sem = {n: SEM(n) for n in ["qs", "qg", "cc", "te", "ve", "se", "ge"]}
    cnt = {n: 0 for n in sem}

    def inc(inst, name, n=1):
        inst.then_inc(sem[name], n)
        cnt[name] += n
        return cnt[name]

    def wt(eng, name, val):
        if val > 0:
            eng.wait_ge(sem[name], val)

    blk_es = contextlib.ExitStack()
    block = blk_es.enter_context(nc.Block())

    def vbar(v, inst):
        val = inc(inst, "ve")
        v.wait_ge(sem["ve"], val)
        return val

    def sbar(s, inst):
        val = inc(inst, "se")
        s.wait_ge(sem["se"], val)
        return val

    def p0_g(g):
        g.memset(ones_r[:, :], 1.0)
        g.memset(ones_c[:, :], 1.0)
        g.memset(smb_sb[:, :], -6.0)
        g.memset(ident_f[:, :], 0.0)
        g.affine_select(out=ident_f[:, :], in_=ident_f[:, :],
                        compare_op=ALU.not_equal, fill=1.0, base=0,
                        pattern=[[-1, 128]], channel_multiplier=1)
        g.memset(ident_b[:, :], 0.0)
        g.affine_select(out=ident_b[:, :], in_=ident_b[:, :],
                        compare_op=ALU.not_equal, fill=1.0, base=0,
                        pattern=[[-1, 128]], channel_multiplier=1)
        inc(g.engine_nop(), "ge")
        inc(g.dma_start(out=router_b[:, :], in_=router.rearrange("p a b -> p (a b)")), "qg", 16)
        inc(g.dma_start(out=rmst[:, :], in_=rmst_in), "qg", 16)
        inc(g.dma_start(out=rmsrow_b[:, :], in_=rmsrow_in), "qg", 16)
        inc(g.dma_start(out=hs_sb[:, :], in_=hsel), "qg", 16)
    block.gpsimd(p0_g)
    ge_consts = cnt["ge"]
    qg_smalls = cnt["qg"]

    p1_load_done = [0] * NT
    p1_var_done = [0] * NT
    p1_rstd_done = [0] * NT
    p1_apply_done = [0] * NT
    p1_grp_tr = [0] * (NT * 4)
    p1_grp_evac = [0] * (NT * 4)

    def emit_p1_load(i):
        def f(sy):
            if i >= 2:
                wt(sy, "ve", p1_apply_done[i - 2])
            p1_load_done[i] = inc(
                sy.dma_start(out=AR2f[:, (i % 2) * 2048:(i % 2) * 2048 + 2048],
                             in_=x_my[i * 128:(i + 1) * 128, :]), "qs", 16)
        block.sync(f)

    def emit_p1_work(i):
        xsl = AR2f[:, (i % 2) * 2048:(i % 2) * 2048 + 2048]
        sqx = AR3f[:, 0:2048]
        ln1tok = AR3[:, 8192 + (i % 2) * 2048: 8192 + (i % 2) * 2048 + 2048]

        def fv(v):
            wt(v, "qs", p1_load_done[i])
            if i >= 1:
                wt(v, "se", p1_rstd_done[i - 1])
            vbar(v, v.reduce_sum(smallf[:, 0:1], xsl, axis=AX.X))
            vbar(v, v.tensor_mul(sqx, xsl, xsl))
            vbar(v, v.reduce_sum(smallf[:, 1:2], sqx, axis=AX.X))
            vbar(v, v.tensor_scalar(out=smallf[:, 2:3], in0=smallf[:, 0:1],
                                    scalar1=1.0 / D, scalar2=None, op0=ALU.mult))
            vbar(v, v.tensor_scalar(out=smallf[:, 1:2], in0=smallf[:, 1:2],
                                    scalar1=1.0 / D, scalar2=None, op0=ALU.mult))
            vbar(v, v.tensor_mul(smallf[:, 3:4], smallf[:, 2:3], smallf[:, 2:3]))
            vbar(v, v.tensor_sub(smallf[:, 3:4], smallf[:, 1:2], smallf[:, 3:4]))
            p1_var_done[i] = vbar(v, v.tensor_scalar(
                out=smallf[:, 3:4], in0=smallf[:, 3:4],
                scalar1=EPS_LN, scalar2=None, op0=ALU.add))
        block.vector(fv)

        def fs(s):
            wt(s, "ve", p1_var_done[i])
            sbar(s, s.activation(out=smallf[:, 4:5], in_=smallf[:, 3:4], func=AF.Ln))
            p1_rstd_done[i] = sbar(s, s.activation(
                out=smallf[:, 4:5], in_=smallf[:, 4:5], func=AF.Exp, scale=-0.5))
        block.scalar(fs)

        def fv2(v):
            wt(v, "se", p1_rstd_done[i])
            vbar(v, v.tensor_mul(smallf[:, 5:6], smallf[:, 2:3], smallf[:, 4:5]))
            vbar(v, v.tensor_scalar(out=smallf[:, 5:6], in0=smallf[:, 5:6],
                                    scalar1=-1.0, scalar2=None, op0=ALU.mult))
            p1_apply_done[i] = vbar(v, v.tensor_scalar(
                out=ln1tok, in0=xsl, scalar1=smallf[:, 4:5],
                scalar2=smallf[:, 5:6], op0=ALU.mult, op1=ALU.add))
        block.vector(fv2)

        for grp in range(4):
            def ft(t, grp=grp):
                if grp == 0:
                    wt(t, "ve", p1_apply_done[i])
                if i == 0 and grp == 0:
                    wt(t, "ge", ge_consts)
                psv = (Q2 if grp % 2 == 0 else Q3)[:, 0:512].bitcast(BF16)
                gi = i * 4 + grp
                if gi >= 2:
                    wt(t, "se", p1_grp_evac[gi - 2])
                for j in range(4):
                    dt = grp * 4 + j
                    last = t.transpose(psv[:, j * 128:(j + 1) * 128],
                                       ln1tok[:, dt * 128:(dt + 1) * 128], ident_b[:, :])
                p1_grp_tr[gi] = inc(last, "te")
            block.tensor(ft)

            def fs2(s, grp=grp):
                dstv = AR1[:, :].rearrange("p (a b) -> p a b", a=16)
                psv = (Q2 if grp % 2 == 0 else Q3)[:, 0:512].bitcast(BF16)
                gi = i * 4 + grp
                wt(s, "te", p1_grp_tr[gi])
                srcv = psv[:, 0:512].rearrange("p (a b) -> p a b", a=4)
                p1_grp_evac[gi] = inc(
                    s.activation(out=dstv[:, grp * 4:(grp + 1) * 4, i * 128:(i + 1) * 128],
                                 in_=srcv[:, :, :], func=AF.Copy), "se")
            block.scalar(fs2)

    emit_p1_load(0)
    emit_p1_load(1)
    for i in range(NT):
        emit_p1_work(i)
        if i + 2 < NT:
            emit_p1_load(i + 2)
    ln1_done_se = p1_grp_evac[NT * 4 - 1]

    if "ln1T" in dbg_out:
        def fdbg(sy):
            wt(sy, "se", ln1_done_se)
            inc(sy.dma_start(out=dbg_out["ln1T"], in_=AR1[:, :]), "qs", 16)
        block.sync(fdbg)

    if upto <= 1:
        def f_final(sy):
            wt(sy, "qs", cnt["qs"])
        block.sync(f_final)
        blk_es.close()
        es.close()
        return nc, dbg_out

    order = list(range(16, 32)) + list(range(0, 16)) + list(range(32, 48))
    NS = 3
    p2_ld_done = [0] * NCT
    p2_mm_done = [0] * NCT
    p2_ev_done = [0] * NCT
    kin_dma = [0] * H
    qT_ap = AR2
    vT_ap = AR4

    def emit_p2_load(idx):
        ct = order[idx]
        def f(g):
            if idx >= NS:
                wt(g, "te", p2_mm_done[idx - NS])
            if idx < 2:
                wt(g, "ve", p1_apply_done[NT - 1])
            sl = S_r[:, (idx % NS) * 2048:(idx % NS) * 2048 + 2048]
            p2_ld_done[idx] = inc(
                g.dma_start(out=sl, in_=wqkv[ct].rearrange("p a b -> p (a b)")),
                "qg", 16)
        block.gpsimd(f)

    def emit_p2_mm(idx):
        ps = Q0 if idx % 2 == 0 else Q1
        sl = S_r[:, (idx % NS) * 2048:(idx % NS) * 2048 + 2048]
        def f(t):
            wt(t, "qg", p2_ld_done[idx])
            if idx == 0:
                wt(t, "se", ln1_done_se)
            if idx >= 2:
                wt(t, "se", p2_ev_done[idx - 2])
            for tc in range(2):
                for dt in range(ND):
                    last = t.matmul(ps[:, tc * 512:(tc + 1) * 512],
                                    sl[:, dt * 128:(dt + 1) * 128],
                                    AR1[:, dt * 1024 + tc * 512: dt * 1024 + tc * 512 + 512],
                                    start=(dt == 0), stop=(dt == ND - 1))
            p2_mm_done[idx] = inc(last, "te")
        block.tensor(f)

    def emit_p2_evac(idx):
        ct = order[idx]
        ps = Q0 if idx % 2 == 0 else Q1
        def f(s):
            wt(s, "te", p2_mm_done[idx])
            if ct < 16:
                dst = qT_ap[:, ct * 1024:(ct + 1) * 1024]
                scale = ISQ
            elif ct < 32:
                if idx >= 2:
                    wt(s, "qs", kin_dma[order[idx - 2] - 16])
                dst = kt_r[:, (idx % 2) * 1024:(idx % 2) * 1024 + 1024]
                scale = 1.0
            else:
                dst = vT_ap[:, (ct - 32) * 1024:(ct - 32 + 1) * 1024]
                scale = 1.0
            p2_ev_done[idx] = inc(
                s.activation(out=dst, in_=ps[:, :], func=AF.Copy, scale=scale), "se")
        block.scalar(f)
        if 16 <= ct < 32:
            h = ct - 16
            def f2(sy):
                wt(sy, "se", p2_ev_done[idx])
                kin_dma[h] = inc(
                    sy.dma_start(out=kin[h * 128:(h + 1) * 128, :],
                                 in_=kt_r[:, (idx % 2) * 1024:(idx % 2) * 1024 + 1024]),
                    "qs", 16)
            block.sync(f2)

    emit_p2_load(0)
    emit_p2_load(1)
    emit_p2_load(2)
    for idx in range(NCT):
        emit_p2_mm(idx)
        emit_p2_evac(idx)
        if idx + NS < NCT:
            emit_p2_load(idx + NS)

    def f_ag(g):
        wt(g, "qs", max(kin_dma))
        inc(g.collective_compute("AllGather", ALU.bypass, replica_groups=PAIRS,
                                 ins=[kin.opt()], outs=[kag.opt()]), "cc")
    block.gpsimd(f_ag)
    cc_ag = cnt["cc"]
    p2_all_ev = cnt["se"]

    for nm, buf in [("qT", qT_ap), ("vT", vT_ap)]:
        if nm in dbg_out:
            def fdbg(sy, buf=buf, nm=nm):
                wt(sy, "se", p2_all_ev)
                inc(sy.dma_start(out=dbg_out[nm], in_=buf[:, :]), "qs", 16)
            block.sync(fdbg)
    if "kag" in dbg_out:
        def fdbg(sy):
            wt(sy, "cc", cc_ag)
            inc(sy.dma_start(out=dbg_out["kag"], in_=kag), "qs", 16)
        block.sync(fdbg)

    wo_ld = [0] * ND
    womm_done = [0] * ND

    def emit_wo_load(ot):
        def f(g):
            idx = NCT + ot
            wt(g, "te", p2_mm_done[NCT - NS + ot] if ot < NS else womm_done[ot - NS])
            sl = S_r[:, (idx % NS) * 2048:(idx % NS) * 2048 + 2048]
            wo_ld[ot] = inc(
                g.dma_start(out=sl, in_=wo[ot].rearrange("p a b -> p (a b)")), "qg", 16)
        block.gpsimd(f)

    emit_wo_load(0)
    emit_wo_load(1)

    if upto <= 2:
        def f_final(sy):
            wt(sy, "qs", cnt["qs"])
        block.sync(f_final)
        blk_es.close()
        es.close()
        return nc, dbg_out

    SMB = -6.0
    p2_q_ev = p2_ev_done[31]
    kf_done = [0] * H
    sc_done = [[0] * 8 for _ in range(H)]
    exp_done = [[0] * 8 for _ in range(H)]
    rr_done = [[0] * 8 for _ in range(H)]
    mv_done = [0] * H
    cevac_done = [0] * H
    cin_done = [0] * H
    cps = [Q0[0:1, 0:512], Q0[0:1, 512:1024], Q1[0:1, 0:512], Q1[0:1, 512:1024]]

    def E_ap(h, qt, c):
        buf = AR3 if h % 2 == 0 else AR1
        return buf[:, qt * 2048 + c * 1024: qt * 2048 + (c + 1) * 1024]

    def emit_kf_load(h):
        def f(sy):
            wt(sy, "cc", cc_ag)
            if h >= 2:
                wt(sy, "te", sc_done[h - 2][7])
            sl = kf_r[:, (h % 2) * 2048:(h % 2) * 2048 + 2048]
            inc(sy.dma_start(out=sl[:, 0:1024],
                             in_=kag[h * 128:(h + 1) * 128, :]), "qs", 16)
            kf_done[h] = inc(
                sy.dma_start(out=sl[:, 1024:2048],
                             in_=kag[2048 + h * 128: 2048 + (h + 1) * 128, :]),
                "qs", 16)
        block.sync(f)

    def emit_p3_rr(h, qt):
        def fv(v):
            wt(v, "se", exp_done[h][qt])
            vbar(v, v.tensor_add(smallf[:, 22 + qt:23 + qt],
                                 smallf[:, 6 + 2 * qt:7 + 2 * qt],
                                 smallf[:, 7 + 2 * qt:8 + 2 * qt]))
            vbar(v, v.reciprocal(smallf[:, 30 + qt:31 + qt],
                                 smallf[:, 22 + qt:23 + qt]))
            rr_done[h][qt] = vbar(v, v.tensor_copy(
                rbuf[:, (h % 2) * 8 + qt:(h % 2) * 8 + qt + 1],
                smallf[:, 30 + qt:31 + qt]))
        block.vector(fv)

    def emit_p3_mv_chunk(h, part):
        def fmv(t, h=h, part=part):
            if part == 0:
                wt(t, "ve", rr_done[h][7])
                if h > 0:
                    wt(t, "ve", cevac_done[h - 1])
            buf = AR3 if h % 2 == 0 else AR1
            for qt in range(2 * part, 2 * part + 2):
                for ch in range(4):
                    last = t.matmul(cps[ch],
                                    rbuf[:, (h % 2) * 8 + qt:(h % 2) * 8 + qt + 1],
                                    buf[:, qt * 2048 + ch * 512: qt * 2048 + ch * 512 + 512],
                                    start=(qt == 0), stop=(qt == 7))
            if part == 3:
                mv_done[h] = inc(last, "te")
        block.tensor(fmv)

    def emit_p3_cev(h):
        def fce(v, h=h):
            wt(v, "te", mv_done[h])
            if h >= 1:
                wt(v, "qs", cin_done[h - 1])
            for ch in range(4):
                v.tensor_copy(rows[0:1, ch * 512:(ch + 1) * 512], cps[ch])
            cevac_done[h] = inc(v.engine_nop(), "ve")
        block.vector(fce)

        def fci(sy, h=h):
            wt(sy, "ve", cevac_done[h])
            cin_done[h] = inc(
                sy.dma_start(out=cin[h:h + 1, :], in_=rows[0:1, 0:2048]), "qs", 16)
        block.sync(fci)

    P3_SEQ = [0, 2, 4, 6, 1, 3, 5, 7]
    P3_PAIR = {0: 0, 2: 0, 4: 0, 6: 0, 1: 1, 3: 0, 5: 1, 7: 0}
    P3_PREV = {0: 7, 2: 0, 4: 2, 6: 4, 1: None, 3: 6, 5: 1, 7: 3}

    def emit_p3_head(h):
        kf = kf_r[:, (h % 2) * 2048:(h % 2) * 2048 + 2048]
        for i, qt in enumerate(P3_SEQ):
            pr = P3_PAIR[qt]
            cA = QQ[2] if pr == 0 else QQ[0]
            cB = QQ[3] if pr == 0 else QQ[1]
            def ft(t, h=h, qt=qt, i=i, cA=cA, cB=cB):
                if i == 0:
                    wt(t, "qs", kf_done[h])
                    if h == 0:
                        wt(t, "se", p2_q_ev)
                if h == 0 and qt == 1:
                    wt(t, "se", p2_ev_done[NCT - 1])
                if qt == 1 and h > 0:
                    wt(t, "ve", cevac_done[h - 1])
                pv = P3_PREV[qt]
                if pv is not None:
                    ph = h - 1 if qt == 0 else h
                    if ph >= 0:
                        wt(t, "se", exp_done[ph][pv])
                for c, ps in ((0, cA), (1, cB)):
                    for sub in range(2):
                        last = t.matmul(
                            ps[:, sub * 512:(sub + 1) * 512],
                            qT_ap[:, h * 1024 + qt * 128: h * 1024 + (qt + 1) * 128],
                            kf[:, c * 1024 + sub * 512: c * 1024 + sub * 512 + 512],
                            start=True, stop=True)
                sc_done[h][qt] = inc(last, "te")
            block.tensor(ft)

            if h > 0 and i <= 3:
                emit_p3_mv_chunk(h - 1, i)
            if h > 0 and i == 3:
                emit_p3_cev(h - 1)

            def fs(s, h=h, qt=qt):
                wt(s, "te", sc_done[h][qt])
                if h >= 1:
                    wt(s, "ve", rr_done[h - 1][qt])
                if qt == 0 and h >= 2:
                    wt(s, "te", mv_done[h - 2])
                if qt == 0 and h == 1:
                    wt(s, "te", p2_mm_done[NCT - 1])
                for c, ps in ((0, cA), (1, cB)):
                    last = s.activation(out=E_ap(h, qt, c), in_=ps[:, :],
                                        func=AF.Exp, bias=smb_sb[:, 0:1],
                                        accum_out=smallf[:, 6 + 2 * qt + c: 7 + 2 * qt + c])
                exp_done[h][qt] = inc(last, "se")
            block.scalar(fs)

            if i >= 1:
                emit_p3_rr(h, P3_SEQ[i - 1])
        emit_p3_rr(h, P3_SEQ[7])

    emit_kf_load(0)
    emit_kf_load(1)
    for h in range(H):
        emit_p3_head(h)
        if h + 2 < H:
            emit_kf_load(h + 2)
    for part in range(4):
        emit_p3_mv_chunk(H - 1, part)
    emit_p3_cev(H - 1)

    if "cin" in dbg_out:
        def fdbg(sy):
            wt(sy, "qs", cin_done[H - 1])
            inc(sy.dma_start(out=dbg_out["cin"], in_=cin), "qs", 16)
        block.sync(fdbg)

    if upto <= 2.7:
        def f_final(sy):
            wt(sy, "qs", cnt["qs"])
        block.sync(f_final)
        blk_es.close()
        es.close()
        return nc, dbg_out

    def f_ar(g):
        wt(g, "qs", cin_done[H - 1])
        inc(g.collective_compute("AllReduce", ALU.add, replica_groups=PAIRS,
                                 ins=[cin.opt()], outs=[car.opt()]), "cc")
    block.gpsimd(f_ar)
    cc_ar = cnt["cc"]

    if "car" in dbg_out:
        def fdbg(sy):
            wt(sy, "cc", cc_ar)
            inc(sy.dma_start(out=dbg_out["car"], in_=car), "qs", 16)
        block.sync(fdbg)

    if upto <= 3:
        def f_final(sy):
            wt(sy, "qs", cnt["qs"])
        block.sync(f_final)
        blk_es.close()
        es.close()
        return nc, dbg_out

    car_ld = [0] * H
    cmy_done = [0] * H
    bc_done = [0] * H
    vmul_done = [0] * H

    def crow2(h):
        return rows[0:1, 0:2048] if h % 2 == 0 else AR2f[0:1, 4096:6144]

    def emit_v_load(h):
        def f(sy):
            wt(sy, "cc", cc_ar)
            if h >= 2:
                wt(sy, "ve", cmy_done[h - 2])
            car_ld[h] = inc(
                sy.dma_start(out=crow2(h), in_=car[h:h + 1, :]), "qs", 16)
        block.sync(f)

    def emit_v_cmy(h):
        def fv(v):
            wt(v, "qs", car_ld[h])
            if h >= 2:
                wt(v, "te", bc_done[h - 2])
            cr = crow2(h)
            vbar(v, v.tensor_scalar(out=rows[0:1, 2048:3072], in0=cr[0:1, 0:1024],
                                    scalar1=hs_sb[0:1, 0:1], scalar2=None, op0=ALU.mult))
            vbar(v, v.scalar_tensor_tensor(out=rows[0:1, 2048:3072],
                                           in0=cr[0:1, 1024:2048],
                                           scalar=hs_sb[0:1, 1:2],
                                           in1=rows[0:1, 2048:3072],
                                           op0=ALU.mult, op1=ALU.add))
            cmy_done[h] = vbar(v, v.tensor_copy(
                chb[0:1, (h % 2) * 1024:(h % 2) * 1024 + 1024], rows[0:1, 2048:3072]))
        block.vector(fv)

    def emit_v_bc(h):
        def ft(t):
            wt(t, "ve", cmy_done[h])
            if h == 0:
                wt(t, "ve", cevac_done[H - 1])
            if h >= 2:
                wt(t, "ve", vmul_done[h - 2])
            qb = Q0 if h % 2 == 0 else Q1
            base = (h % 2) * 1024
            t.matmul(qb[:, 0:512], ones_r[0:1, :],
                     chb[0:1, base:base + 512], start=True, stop=True)
            last = t.matmul(qb[:, 512:1024], ones_r[0:1, :],
                            chb[0:1, base + 512:base + 1024], start=True, stop=True)
            bc_done[h] = inc(last, "te")
        block.tensor(ft)

    def emit_v_mul(h):
        def fv2(v):
            wt(v, "te", bc_done[h])
            qb = Q0 if h % 2 == 0 else Q1
            vmul_done[h] = inc(
                v.tensor_tensor(vT_ap[:, h * 1024:(h + 1) * 1024],
                                vT_ap[:, h * 1024:(h + 1) * 1024],
                                qb[:, :], op=ALU.mult), "ve")
        block.vector(fv2)

    emit_v_load(0)
    emit_v_cmy(0)
    for h in range(H):
        if h + 1 < H:
            emit_v_load(h + 1)
        emit_v_bc(h)
        if h + 1 < H:
            emit_v_cmy(h + 1)
        emit_v_mul(h)
    values_done = vmul_done[H - 1]

    if "values" in dbg_out:
        def fdbg(sy):
            wt(sy, "ve", values_done)
            inc(sy.dma_start(out=dbg_out["values"], in_=vT_ap[:, :]), "qs", 16)
        block.sync(fdbg)

    if upto <= 4:
        def f_final(sy):
            wt(sy, "qs", cnt["qs"])
        block.sync(f_final)
        blk_es.close()
        es.close()
        return nc, dbg_out

    xc_ld = [0] * ND
    tr_done = [0] * ND
    xev_done = [0] * ND
    add_done = [0] * ND

    def emit_xc_load(ot):
        def f(sy):
            if ot >= 2:
                wt(sy, "te", tr_done[ot - 2])
            src = x_my[:, ot * 128:(ot + 1) * 128].rearrange("(i p) j -> p i j", p=128)
            xc_ld[ot] = inc(
                sy.dma_start(out=xc_r[:, (ot % 2) * 1024:(ot % 2) * 1024 + 1024],
                             in_=src), "qs", 16)
        block.sync(f)

    def emit_p4(ot):
        idx = NCT + ot
        sl = S_r[:, (idx % NS) * 2048:(idx % NS) * 2048 + 2048]
        xcs = xc_r[:, (ot % 2) * 1024:(ot % 2) * 1024 + 1024]

        trQ = Q0 if ot % 2 == 0 else Q2
        woQ = Q1 if ot % 2 == 0 else Q3

        def ft(t):
            wt(t, "qs", xc_ld[ot])
            wt(t, "qg", wo_ld[ot])
            if ot <= 1:
                wt(t, "ve", values_done)
            if ot >= 2:
                wt(t, "ve", xev_done[ot - 2])
            for j in range(8):
                last = t.transpose(trQ[:, j * 128:(j + 1) * 128],
                                   xcs[:, j * 128:(j + 1) * 128], ident_f[:, :])
            tr_done[ot] = inc(last, "te")
            if ot >= 2:
                wt(t, "ve", add_done[ot - 2])
            for tc in range(2):
                for ft_ in range(ND):
                    last = t.matmul(woQ[:, tc * 512:(tc + 1) * 512],
                                    sl[:, ft_ * 128:(ft_ + 1) * 128],
                                    vT_ap[:, ft_ * 1024 + tc * 512: ft_ * 1024 + tc * 512 + 512],
                                    start=(ft_ == 0), stop=(ft_ == ND - 1))
            womm_done[ot] = inc(last, "te")
        block.tensor(ft)

        def fv(v):
            wt(v, "te", tr_done[ot])
            xev_done[ot] = inc(v.tensor_copy(xrT[:, :], trQ[:, :]), "ve")
            wt(v, "te", womm_done[ot])
            v.tensor_add(x1T(ot, 0, 512), woQ[:, 0:512], xrT[:, 0:512])
            add_done[ot] = inc(
                v.tensor_add(x1T(ot, 512, 512), woQ[:, 512:1024], xrT[:, 512:1024]), "ve")
        block.vector(fv)

    emit_xc_load(0)
    emit_xc_load(1)
    for ot in range(ND):
        emit_p4(ot)
        if ot + 2 < ND:
            emit_wo_load(ot + 2)
            emit_xc_load(ot + 2)
    x1_done = add_done[ND - 1]

    if "x1T" in dbg_out:
        def fdbg(sy):
            wt(sy, "ve", x1_done)
            inc(sy.dma_start(out=dbg_out["x1T"][:, 0:8192], in_=AR2f[:, :]), "qs", 16)
            inc(sy.dma_start(out=dbg_out["x1T"][:, 8192:16384], in_=AR3f[:, :]), "qs", 16)
        block.sync(fdbg)

    if upto <= 5:
        def f_final(sy):
            wt(sy, "qs", cnt["qs"])
        block.sync(f_final)
        blk_es.close()
        es.close()
        return nc, dbg_out

    cast_done = [0] * ND
    sq_done = [0] * ND
    stat_mm = [0] * ND

    for dt in range(ND):
        def fv(v, dt=dt):
            wt(v, "ve", add_done[dt])
            if dt == 0:
                wt(v, "te", womm_done[ND - 1])
            cast_done[dt] = inc(
                v.tensor_copy(AR4[:, dt * 1024:(dt + 1) * 1024], x1T(dt)), "ve")
            if dt >= 2:
                wt(v, "te", stat_mm[dt - 2])
            if dt == 0:
                wt(v, "se", exp_done[H - 1][7])
                wt(v, "te", mv_done[H - 1])
            sq_done[dt] = inc(v.tensor_mul(
                AR1[:, (dt % 2) * 1024:(dt % 2) * 1024 + 1024],
                x1T(dt), x1T(dt)), "ve")
        block.vector(fv)

        def ft(t, dt=dt):
            wt(t, "ve", sq_done[dt])
            if dt == 0:
                wt(t, "ve", cevac_done[H - 1])
                wt(t, "ve", add_done[ND - 1])
            sqs = AR1[:, (dt % 2) * 1024:(dt % 2) * 1024 + 1024]
            x1b = AR4[:, dt * 1024:(dt + 1) * 1024]
            for ch in range(2):
                t.matmul(Q0[0:1, ch * 512:(ch + 1) * 512], ones_c[:, 0:1],
                         x1b[:, ch * 512:(ch + 1) * 512],
                         start=(dt == 0), stop=(dt == ND - 1))
                last = t.matmul(Q1[0:1, ch * 512:(ch + 1) * 512], ones_c[:, 0:1],
                                sqs[:, ch * 512:(ch + 1) * 512],
                                start=(dt == 0), stop=(dt == ND - 1))
            stat_mm[dt] = inc(last, "te")
        block.tensor(ft)
    stats_mm_done = stat_mm[ND - 1]

    if "x1b" in dbg_out:
        def fdbg(sy):
            wt(sy, "ve", cast_done[ND - 1])
            inc(sy.dma_start(out=dbg_out["x1b"], in_=AR4[:, :]), "qs", 16)
        block.sync(fdbg)

    if upto <= 6:
        def f_final(sy):
            wt(sy, "qs", cnt["qs"])
        block.sync(f_final)
        blk_es.close()
        es.close()
        return nc, dbg_out

    A_ = rows[0:1, 0:1024]
    Bs_ = rows[0:1, 1024:2048]
    C_ = rows[0:1, 2048:3072]
    T_ = rows[0:1, 3072:4096]

    def f_statev(s):
        wt(s, "te", stats_mm_done)
        s.activation(out=A_[0:1, 0:512], in_=Q0[0:1, 0:512], func=AF.Copy)
        s.activation(out=A_[0:1, 512:1024], in_=Q0[0:1, 512:1024], func=AF.Copy)
        s.activation(out=Bs_[0:1, 0:512], in_=Q1[0:1, 0:512], func=AF.Copy)
        sbar(s, s.activation(out=Bs_[0:1, 512:1024], in_=Q1[0:1, 512:1024],
                             func=AF.Copy))
        stat_ev = cnt["se"]
        return stat_ev
    block.scalar(f_statev)
    stat_ev = cnt["se"]

    def f_rows1(v):
        wt(v, "se", stat_ev)
        vbar(v, v.tensor_scalar(out=C_, in0=A_, scalar1=1.0 / D, scalar2=None,
                                op0=ALU.mult))
        vbar(v, v.tensor_scalar(out=A_, in0=Bs_, scalar1=1.0 / D, scalar2=None,
                                op0=ALU.mult))
        vbar(v, v.tensor_mul(T_, C_, C_))
        vbar(v, v.tensor_sub(A_, A_, T_))
        vbar(v, v.tensor_scalar(out=Bs_, in0=A_, scalar1=EPS_LN, scalar2=None,
                                op0=ALU.add))
    block.vector(f_rows1)
    rows1_ve = cnt["ve"]

    def f_rows2(s):
        wt(s, "ve", rows1_ve)
        sbar(s, s.activation(out=T_, in_=Bs_, func=AF.Ln))
        sbar(s, s.activation(out=Bs_, in_=T_, func=AF.Exp, scale=-0.5))
    block.scalar(f_rows2)
    rows2_se = cnt["se"]

    def f_rows3(v):
        wt(v, "se", rows2_se)
        vbar(v, v.tensor_mul(T_, Bs_, Bs_))
        vbar(v, v.tensor_mul(T_, A_, T_))
        vbar(v, v.tensor_scalar(out=T_, in0=T_, scalar1=EPS_RMS, scalar2=None,
                                op0=ALU.add))
    block.vector(f_rows3)
    rows3_ve = cnt["ve"]

    def f_rows4(s):
        wt(s, "ve", rows3_ve)
        sbar(s, s.activation(out=A_, in_=T_, func=AF.Ln))
        sbar(s, s.activation(out=T_, in_=A_, func=AF.Exp, scale=-0.5))
    block.scalar(f_rows4)
    rows4_se = cnt["se"]

    def f_rows5(v):
        wt(v, "se", rows4_se)
        vbar(v, v.tensor_mul(A_, Bs_, T_))
        vbar(v, v.tensor_copy(sigb[0:1, :], A_))
        vbar(v, v.tensor_mul(T_, C_, A_))
        vbar(v, v.tensor_scalar(out=T_, in0=T_, scalar1=-1.0, scalar2=None,
                                op0=ALU.mult))
        vbar(v, v.tensor_copy(nmsb[0:1, :], T_))
    block.vector(f_rows5)
    rows5_ve = cnt["ve"]

    if "stats" in dbg_out:
        def fdbg(sy):
            wt(sy, "ve", rows5_ve)
            inc(sy.dma_start(out=dbg_out["stats"], in_=rows[0:1, :]), "qs", 16)
        block.sync(fdbg)

    if upto <= 7:
        def f_final(sy):
            wt(sy, "qs", cnt["qs"])
        block.sync(f_final)
        blk_es.close()
        es.close()
        return nc, dbg_out

    def f_sbc(t):
        wt(t, "ve", rows5_ve)
        wt(t, "ve", add_done[ND - 1])
        wt(t, "qg", qg_smalls)
        t.matmul(Q3[:, 0:512], ones_r[0:1, :], sigb[0:1, 0:512], start=True, stop=True)
        last = t.matmul(Q3[:, 512:1024], ones_r[0:1, :], sigb[0:1, 512:1024],
                        start=True, stop=True)
        inc(last, "te")
    block.tensor(f_sbc)
    sbc_te = cnt["te"]

    outer_done = [0] * ND
    h2_done = [0] * ND
    for dt in range(ND):
        def ft(t, dt=dt):
            if dt == 0:
                wt(t, "se", stat_ev)
            if dt >= 2:
                wt(t, "ve", h2_done[dt - 2])
            half = Q0 if dt % 2 == 0 else Q1
            t.matmul(half[:, 0:512], rmsrow_b[0:1, dt * 128:(dt + 1) * 128],
                     nmsb[0:1, 0:512], start=True, stop=True)
            last = t.matmul(half[:, 512:1024], rmsrow_b[0:1, dt * 128:(dt + 1) * 128],
                            nmsb[0:1, 512:1024], start=True, stop=True)
            outer_done[dt] = inc(last, "te")
        block.tensor(ft)

        def fv(v, dt=dt):
            wt(v, "te", sbc_te)
            wt(v, "te", outer_done[dt])
            if dt < 2:
                wt(v, "te", stat_mm[ND - 1])
            h2sl = AR1[:, dt * 1024:(dt + 1) * 1024]
            half = Q0 if dt % 2 == 0 else Q1
            vbar(v, v.scalar_tensor_tensor(out=h2sl, in0=x1T(dt),
                                           scalar=rmst[:, dt:dt + 1], in1=Q3[:, :],
                                           op0=ALU.mult, op1=ALU.mult))
            h2_done[dt] = inc(
                v.tensor_tensor(h2sl, h2sl, half[:, :], op=ALU.add), "ve")
        block.vector(fv)
    h2_all = h2_done[ND - 1]

    if "h2T" in dbg_out:
        def fdbg(sy):
            wt(sy, "ve", h2_all)
            inc(sy.dma_start(out=dbg_out["h2T"], in_=AR1[:, :]), "qs", 16)
        block.sync(fdbg)

    if upto <= 8:
        def f_final(sy):
            wt(sy, "qs", cnt["qs"])
        block.sync(f_final)
        blk_es.close()
        es.close()
        return nc, dbg_out

    lg_done = [0] * NT
    lgev_done = [0] * NT
    gate_done = [0] * NT

    for i in range(NT):
        def ft(t, i=i):
            wt(t, "ve", h2_all)
            if i >= 1:
                wt(t, "ve", lgev_done[i - 1])
            for dt in range(ND):
                last = t.matmul(Q0[:, 0:8],
                                AR1[:, dt * 1024 + i * 128: dt * 1024 + (i + 1) * 128],
                                router_b[:, dt * 8:(dt + 1) * 8],
                                start=(dt == 0), stop=(dt == ND - 1))
            lg_done[i] = inc(last, "te")
        block.tensor(ft)

        def fv(v, i=i):
            wt(v, "te", lg_done[i])
            lgev_done[i] = vbar(v, v.tensor_copy(logit_sb[:, :], Q0[:, 0:8]))
            vbar(v, v.reduce_max(gscr[:, 0:1], logit_sb[:, :], axis=AX.X))
            vbar(v, v.tensor_scalar(out=gscr[:, 8:16], in0=logit_sb[:, :],
                                    scalar1=gscr[:, 0:1], scalar2=None,
                                    op0=ALU.is_equal))
            vbar(v, v.tensor_scalar(out=gscr[:, 8:16], in0=gscr[:, 8:16],
                                    scalar1=2e30, scalar2=None, op0=ALU.mult))
            vbar(v, v.tensor_sub(gscr[:, 8:16], logit_sb[:, :], gscr[:, 8:16]))
            vbar(v, v.reduce_max(gscr[:, 1:2], gscr[:, 8:16], axis=AX.X))
            vbar(v, v.tensor_scalar(out=gscr[:, 2:3], in0=gscr[:, 0:1],
                                    scalar1=-1.0, scalar2=None, op0=ALU.mult))
        block.vector(fv)
        g_m_ve = cnt["ve"]

        def fs(s, i=i, g_m_ve=g_m_ve):
            wt(s, "ve", g_m_ve)
            s.activation(out=gscr[:, 3:4], in_=gscr[:, 1:2], func=AF.Exp,
                         bias=gscr[:, 2:3]).then_inc(sem["se"], 1)
        block.scalar(fs)
        cnt["se"] += 1
        g_e2_se = cnt["se"]

        def fv2(v, i=i, g_e2_se=g_e2_se):
            wt(v, "se", g_e2_se)
            vbar(v, v.tensor_scalar(out=gscr[:, 4:5], in0=gscr[:, 3:4],
                                    scalar1=1.0, scalar2=None, op0=ALU.add))
            vbar(v, v.reciprocal(gscr[:, 5:6], gscr[:, 4:5]))
            vbar(v, v.tensor_scalar(out=gcolb[:, i * 8:(i + 1) * 8],
                                    in0=logit_sb[:, :], scalar1=gscr[:, 0:1],
                                    scalar2=gscr[:, 5:6], op0=ALU.is_equal,
                                    op1=ALU.mult))
            vbar(v, v.tensor_mul(gscr[:, 3:4], gscr[:, 3:4], gscr[:, 5:6]))
            vbar(v, v.tensor_scalar(out=gscr[:, 8:16], in0=logit_sb[:, :],
                                    scalar1=gscr[:, 1:2], scalar2=gscr[:, 3:4],
                                    op0=ALU.is_equal, op1=ALU.mult))
            gate_done[i] = vbar(v, v.tensor_add(gcolb[:, i * 8:(i + 1) * 8],
                                                gcolb[:, i * 8:(i + 1) * 8],
                                                gscr[:, 8:16]))
        block.vector(fv2)
    gates_all = gate_done[NT - 1]

    if "gates" in dbg_out:
        def fdbg(sy):
            wt(sy, "ve", gates_all)
            inc(sy.dma_start(out=dbg_out["gates"], in_=gcolb[:, :]), "qs", 16)
        block.sync(fdbg)

    grow_ev = [0] * 8
    for e in range(8):
        def ft(t, e=e):
            wt(t, "ve", gates_all)
            if e == 0:
                wt(t, "ve", h2_done[ND - 1])
            else:
                wt(t, "se", grow_ev[e - 1])
            for i in range(NT):
                last = t.matmul(Q1[0:1, i * 128:(i + 1) * 128],
                                gcolb[:, i * 8 + e: i * 8 + e + 1], ident_b[:, :],
                                start=True, stop=True)
            inc(last, "te")
        block.tensor(ft)
        g_row_te = cnt["te"]

        def fs(s, e=e, g_row_te=g_row_te):
            wt(s, "te", g_row_te)
            grow_ev[e] = inc(
                s.activation(out=growsb[0:1, e * 1024:(e + 1) * 1024],
                             in_=Q1[0:1, 0:1024], func=AF.Copy), "se")
        block.scalar(fs)
    grows_all = grow_ev[7]

    if "grows" in dbg_out:
        def fdbg(sy):
            wt(sy, "se", grows_all)
            inc(sy.dma_start(out=dbg_out["grows"], in_=growsb[0:1, 0:8192]), "qs", 16)
        block.sync(fdbg)

    if upto <= 9:
        def f_final(sy):
            wt(sy, "qs", cnt["qs"])
        block.sync(f_final)
        blk_es.close()
        es.close()
        return nc, dbg_out

    ngu = NE * NF * 2
    gu_ld = [0] * ngu
    gu_mm = [0] * ngu
    guev = [0] * (NE * NF)
    fold_done = [0] * (NE * NF)
    bcg_done = [0] * NE
    wd_ld = [0] * (NE * ND)
    dn_mm = [0] * (NE * ND)
    dn_add = [0] * (NE * ND)

    def emit_gu_load(j):
        e, r = divmod(j, 2 * NF)
        ft_, gu = divmod(r, 2)
        src = (wg_in if gu == 0 else wu_in)[e, ft_]
        def f(g):
            if j < NS:
                wt(g, "te", womm_done[ND - 1])
            else:
                wt(g, "te", gu_mm[j - NS])
            sl = S_r[:, (j % NS) * 2048:(j % NS) * 2048 + 2048]
            gu_ld[j] = inc(
                g.dma_start(out=sl, in_=src.rearrange("p a b -> p (a b)")), "qg", 16)
        block.gpsimd(f)

    def emit_wd_load(e, dt):
        j = e * ND + dt
        def f(g):
            if j >= 4:
                wt(g, "te", dn_mm[j - 4])
            sl = wd_r[:, (j % 4) * 1024:(j % 4) * 1024 + 1024]
            wd_ld[j] = inc(
                g.dma_start(out=sl, in_=wd_in[e, dt].rearrange("p a b -> p (a b)")),
                "qg", 16)
        block.gpsimd(f)

    gbc_done = [0] * NE

    def emit_bcg(e):
        def ft(t):
            wt(t, "se", grows_all)
            if e >= 1:
                wt(t, "ve", dn_add[(e - 1) * ND + ND - 1])
            t.matmul(Q2[:, 0:512], ones_r[0:1, :],
                     growsb[0:1, e * 1024:e * 1024 + 512], start=True, stop=True)
            last = t.matmul(Q2[:, 512:1024], ones_r[0:1, :],
                            growsb[0:1, e * 1024 + 512:(e + 1) * 1024],
                            start=True, stop=True)
            bcg_done[e] = inc(last, "te")
        block.tensor(ft)

        def fvc(v):
            wt(v, "te", bcg_done[e])
            gbc_done[e] = inc(v.tensor_copy(gbc_sb[:, :], Q2[:, :]), "ve")
        block.vector(fvc)

    def emit_gu(e, ft_):
        jg = (e * NF + ft_) * 2
        ju = jg + 1
        k = e * NF + ft_

        gQ = Q0 if ft_ % 2 == 0 else Q2
        uQ = Q1 if ft_ % 2 == 0 else Q3
        sgo = 8192 + (ft_ % 2) * 2048
        uo = sgo + 1024

        def ft(t):
            wt(t, "qg", gu_ld[ju])
            if ft_ == 0 and e == 0:
                wt(t, "se", grow_ev[7])
                wt(t, "ve", h2_all)
            if ft_ == 1 and e >= 1:
                wt(t, "ve", dn_add[(e - 1) * ND + ND - 1])
            if ft_ == 1 and e < 8:
                wt(t, "ve", gbc_done[e])
            if k >= 2:
                wt(t, "se", guev[k - 2])
            slg = S_r[:, (jg % NS) * 2048:(jg % NS) * 2048 + 2048]
            slu = S_r[:, (ju % NS) * 2048:(ju % NS) * 2048 + 2048]
            for tc in range(2):
                for dt in range(ND):
                    last = t.matmul(gQ[:, tc * 512:(tc + 1) * 512],
                                    slg[:, dt * 128:(dt + 1) * 128],
                                    AR1[:, dt * 1024 + tc * 512: dt * 1024 + tc * 512 + 512],
                                    start=(dt == 0), stop=(dt == ND - 1))
            gu_mm[jg] = inc(last, "te")
            for tc in range(2):
                for dt in range(ND):
                    last = t.matmul(uQ[:, tc * 512:(tc + 1) * 512],
                                    slu[:, dt * 128:(dt + 1) * 128],
                                    AR1[:, dt * 1024 + tc * 512: dt * 1024 + tc * 512 + 512],
                                    start=(dt == 0), stop=(dt == ND - 1))
            gu_mm[ju] = inc(last, "te")
        block.tensor(ft)

        def fs(s):
            wt(s, "te", gu_mm[ju])
            if k == 0:
                wt(s, "ve", cast_done[ND - 1])
                wt(s, "te", stats_mm_done)
            if k >= 2:
                wt(s, "ve", fold_done[k - 2])
            s.activation(out=AR4[:, sgo:sgo + 1024], in_=gQ[:, :], func=AF.Silu)
            guev[k] = inc(
                s.activation(out=AR4[:, uo:uo + 1024], in_=uQ[:, :],
                             func=AF.Copy), "se")
        block.scalar(fs)

        def fv(v):
            wt(v, "se", guev[k])
            if e < 8 and ft_ == 0:
                wt(v, "ve", gbc_done[e])
            if e >= 1 and ft_ == 0:
                wt(v, "te", dn_mm[(e - 1) * ND + ND - 1])
            asl = AR4[:, ft_ * 1024:(ft_ + 1) * 1024]
            vbar(v, v.tensor_mul(asl, AR4[:, sgo:sgo + 1024], AR4[:, uo:uo + 1024]))
            if e < 8:
                fold_done[k] = vbar(v, v.tensor_tensor(asl, asl, gbc_sb[:, :],
                                                       op=ALU.mult))
            else:
                fold_done[k] = cnt["ve"]
        block.vector(fv)

    def emit_down(e, dt):
        j = e * ND + dt

        dQ = Q2 if dt % 2 == 0 else Q3

        def ft(t):
            wt(t, "qg", wd_ld[j])
            wt(t, "ve", fold_done[e * NF + NF - 1])
            if j >= 2:
                wt(t, "ve", dn_add[j - 2])
            sl = wd_r[:, (j % 4) * 1024:(j % 4) * 1024 + 1024]
            for tc in range(2):
                for ft_ in range(NF):
                    last = t.matmul(dQ[:, tc * 512:(tc + 1) * 512],
                                    sl[:, ft_ * 128:(ft_ + 1) * 128],
                                    AR4[:, ft_ * 1024 + tc * 512: ft_ * 1024 + tc * 512 + 512],
                                    start=(ft_ == 0), stop=(ft_ == NF - 1))
            dn_mm[j] = inc(last, "te")
        block.tensor(ft)

        def fv(v):
            wt(v, "te", dn_mm[j])
            v.tensor_add(x1T(dt, 0, 512), dQ[:, 0:512], x1T(dt, 0, 512))
            dn_add[j] = inc(
                v.tensor_add(x1T(dt, 512, 512), dQ[:, 512:1024], x1T(dt, 512, 512)), "ve")
        block.vector(fv)

    emit_gu_load(0)
    emit_gu_load(1)
    emit_gu_load(2)
    next_gu_ld = 3
    next_wd = [0]

    def emit_wd_next():
        j = next_wd[0]
        if j < NE * ND:
            emit_wd_load(j // ND, j % ND)
            next_wd[0] += 1

    for _ in range(4):
        emit_wd_next()
    for e in range(NE):
        if e < 8:
            emit_bcg(e)
        for ft_ in range(NF):
            emit_gu(e, ft_)
            for _ in range(2):
                if next_gu_ld < ngu:
                    emit_gu_load(next_gu_ld)
                    next_gu_ld += 1
        for dt in range(ND):
            emit_down(e, dt)
            emit_wd_next()
    ffn_done = dn_add[NE * ND - 1]
    last_guev = guev[NE * NF - 1]

    if upto <= 10:
        def f_final(sy):
            wt(sy, "qs", cnt["qs"])
        block.sync(f_final)
        blk_es.close()
        es.close()
        return nc, dbg_out

    p7_tr = [0] * NT
    p7_ev = [0] * NT
    p7_st = [0] * NT

    for i in range(NT):
        pA = Q0 if i % 2 == 0 else Q2
        pB = Q1 if i % 2 == 0 else Q3
        def ft(t, i=i, pA=pA, pB=pB):
            wt(t, "ve", ffn_done)
            if i == 0:
                wt(t, "se", last_guev)
            if i >= 2:
                wt(t, "ve", p7_ev[i - 2])
            for dt in range(ND):
                tgt = pA if dt < 8 else pB
                last = t.transpose(tgt[:, (dt % 8) * 128:((dt % 8) + 1) * 128],
                                   x1T(dt, i * 128, 128), ident_f[:, :])
            p7_tr[i] = inc(last, "te")
        block.tensor(ft)

        def fv(v, i=i, pA=pA, pB=pB):
            wt(v, "te", p7_tr[i])
            if i >= 2:
                wt(v, "qs", p7_st[i - 2])
            if i == 0:
                wt(v, "ve", fold_done[NE * NF - 1])
            osl = AR4f[:, (i % 2) * 2048:(i % 2) * 2048 + 2048]
            v.tensor_copy(osl[:, 0:1024], pA[:, :])
            p7_ev[i] = inc(v.tensor_copy(osl[:, 1024:2048], pB[:, :]), "ve")
        block.vector(fv)

        def fy(sy, i=i):
            wt(sy, "ve", p7_ev[i])
            osl = AR4f[:, (i % 2) * 2048:(i % 2) * 2048 + 2048]
            p7_st[i] = inc(
                sy.dma_start(out=o_my[i * 128:(i + 1) * 128, :], in_=osl), "qs", 16)
        block.sync(fy)

    def f_final(sy):
        wt(sy, "qs", cnt["qs"])
    block.sync(f_final)

    blk_es.close()
    es.close()
    return nc, dbg_out


def _prep_weights(L0_attn_w_qkv, L0_attn_w_o, L0_ffn_rms_weight, L0_ffn_router_w,
                  L0_ffn_w_gate, L0_ffn_w_up, L0_ffn_w_down,
                  L0_ffn_sw_gate, L0_ffn_sw_up, L0_ffn_sw_down):
    f = np.float32
    wqkv = np.ascontiguousarray(
        np.asarray(L0_attn_w_qkv, f).reshape(16, 128, 48, 128).transpose(2, 1, 0, 3))
    wo = np.ascontiguousarray(
        np.asarray(L0_attn_w_o, f).reshape(16, 128, 16, 128).transpose(2, 1, 0, 3))
    router = np.ascontiguousarray(
        np.asarray(L0_ffn_router_w, f).reshape(16, 128, 8).transpose(1, 0, 2))
    rmst = np.ascontiguousarray(np.asarray(L0_ffn_rms_weight, f).reshape(16, 128).T)
    rmsrow = np.ascontiguousarray(np.asarray(L0_ffn_rms_weight, f).reshape(1, 2048))
    wg8 = np.asarray(L0_ffn_w_gate, f).reshape(8, 16, 128, 8, 128).transpose(0, 3, 2, 1, 4)
    swg = np.asarray(L0_ffn_sw_gate, f).reshape(16, 128, 8, 128).transpose(2, 1, 0, 3)
    wg = np.ascontiguousarray(np.concatenate([wg8, swg[None]], axis=0))
    wu8 = np.asarray(L0_ffn_w_up, f).reshape(8, 16, 128, 8, 128).transpose(0, 3, 2, 1, 4)
    swu = np.asarray(L0_ffn_sw_up, f).reshape(16, 128, 8, 128).transpose(2, 1, 0, 3)
    wu = np.ascontiguousarray(np.concatenate([wu8, swu[None]], axis=0))
    wd8 = np.asarray(L0_ffn_w_down, f).reshape(8, 8, 128, 16, 128).transpose(0, 3, 2, 1, 4)
    swd = np.asarray(L0_ffn_sw_down, f).reshape(8, 128, 16, 128).transpose(2, 1, 0, 3)
    wd = np.ascontiguousarray(np.concatenate([wd8, swd[None]], axis=0))
    return dict(wqkv=wqkv, wo=wo, router=router, rmst_in=rmst, rmsrow_in=rmsrow,
                wg_in=wg, wu_in=wu, wd_in=wd)


def make_in_maps(inputs):
    x2d = np.ascontiguousarray(np.asarray(inputs["L0_attn_x"], np.float32).reshape(8192, 2048))
    wmap = _prep_weights(
        inputs["L0_attn_w_qkv"], inputs["L0_attn_w_o"], inputs["L0_ffn_rms_weight"],
        inputs["L0_ffn_router_w"], inputs["L0_ffn_w_gate"], inputs["L0_ffn_w_up"],
        inputs["L0_ffn_w_down"], inputs["L0_ffn_sw_gate"], inputs["L0_ffn_sw_up"],
        inputs["L0_ffn_sw_down"])
    in_maps = []
    for c in range(NCORE):
        m = dict(wmap)
        m["x_my"] = np.ascontiguousarray(x2d[c * TPC:(c + 1) * TPC])
        m["hsel"] = np.array([[1.0, 0.0]] if c % 2 == 0 else [[0.0, 1.0]], np.float32)
        in_maps.append(m)
    return in_maps


def kernel(**inputs):
    in_maps = make_in_maps(inputs)
    nc, _ = build_program()
    res = run_bass_kernel_spmd(nc, in_maps, core_ids=list(range(NCORE)))
    out = np.concatenate([res.results[c]["o_my"] for c in range(NCORE)], axis=0)
    return np.ascontiguousarray(out.reshape(4, 2048, 2048))


# version 08 (final)
# speedup vs baseline: 1.0929x; 1.0929x over previous
import contextlib
import numpy as np
import ml_dtypes
import concourse.bass as bass
import concourse.mybir as mybir
from concourse.bass_utils import run_bass_kernel_spmd

F32 = mybir.dt.float32
BF16 = mybir.dt.bfloat16
AF = mybir.ActivationFunctionType
ALU = mybir.AluOpType
AX = mybir.AxisListType

NCORE = 8
TPC = 1024
NT = 8
D = 2048
ND = 16
H = 16
NCT = 48
NE = 9
NF = 8
ISQ = float(1.0 / np.sqrt(128.0))
EPS_LN = 1e-5
EPS_RMS = 1e-6
PAIRS = [[0, 1], [2, 3], [4, 5], [6, 7]]


def build_program(dbg=(), upto=99):
    dbg = set(dbg)
    nc = bass.Bass("TRN2", target_bir_lowering=False, debug=False)

    x_my = nc.dram_tensor("x_my", [TPC, D], F32, kind="ExternalInput").ap()
    wqkv = nc.dram_tensor("wqkv", [NCT, 128, ND, 128], F32, kind="ExternalInput").ap()
    wo = nc.dram_tensor("wo", [ND, 128, ND, 128], F32, kind="ExternalInput").ap()
    router = nc.dram_tensor("router", [128, ND, 8], F32, kind="ExternalInput").ap()
    rmst_in = nc.dram_tensor("rmst_in", [128, ND], F32, kind="ExternalInput").ap()
    rmsrow_in = nc.dram_tensor("rmsrow_in", [1, D], F32, kind="ExternalInput").ap()
    wg_in = nc.dram_tensor("wg_in", [NE, NF, 128, ND, 128], F32, kind="ExternalInput").ap()
    wu_in = nc.dram_tensor("wu_in", [NE, NF, 128, ND, 128], F32, kind="ExternalInput").ap()
    wd_in = nc.dram_tensor("wd_in", [NE, ND, 128, NF, 128], F32, kind="ExternalInput").ap()
    hsel = nc.dram_tensor("hsel", [1, 2], F32, kind="ExternalInput").ap()
    o_my = nc.dram_tensor("o_my", [TPC, D], F32, kind="ExternalOutput").ap()

    kin = nc.dram_tensor("kin", [H * 128, TPC], BF16, kind="Internal").ap()
    kag = nc.dram_tensor("kag", [2 * H * 128, TPC], BF16, kind="Internal").ap()
    cin = nc.dram_tensor("cin", [H, 2048], F32, kind="Internal").ap()
    car = nc.dram_tensor("car", [H, 2048], F32, kind="Internal").ap()

    dbg_out = {}
    def mkdbg(name, shape, dt):
        if name in dbg:
            dbg_out[name] = nc.dram_tensor("dbg_" + name, shape, dt, kind="ExternalOutput").ap()
    mkdbg("ln1T", [128, ND * TPC], BF16)
    mkdbg("qT", [128, H * TPC], BF16)
    mkdbg("vT", [128, H * TPC], BF16)
    mkdbg("kag", [2 * H * 128, TPC], BF16)
    mkdbg("cin", [H, 2048], F32)
    mkdbg("car", [H, 2048], F32)
    mkdbg("values", [128, H * TPC], BF16)
    mkdbg("x1T", [128, ND * TPC], F32)
    mkdbg("x1b", [128, ND * TPC], BF16)
    mkdbg("stats", [1, 4096], F32)
    mkdbg("h2T", [128, ND * TPC], BF16)
    mkdbg("gates", [128, 64], BF16)
    mkdbg("grows", [1, 8192], BF16)

    es = contextlib.ExitStack()
    SB = lambda n, s, d: es.enter_context(nc.sbuf_tensor(n, s, d))
    PS = lambda n, s, d: es.enter_context(nc.psum_tensor(n, s, d))
    SEM = lambda n: es.enter_context(nc.semaphore(n))

    AR1 = SB("AR1", [128, 16384], BF16)
    AR2 = SB("AR2", [128, 16384], BF16)
    AR3 = SB("AR3", [128, 16384], BF16)
    AR4 = SB("AR4", [128, 16384], BF16)
    kf_r = SB("kf_r", [128, 2 * 2048], BF16)
    kt_r = SB("kt_r", [128, 2 * 1024], BF16)
    S_r = SB("S_r", [128, 3 * 2048], BF16)
    wd_r = SB("wd_r", [128, 4 * 1024], BF16)
    xc_r = SB("xc_r", [128, 2 * 1024], F32)
    xrT = SB("xrT", [128, 1024], F32)
    rows = SB("rows", [1, 4096], F32)
    sigb = SB("sigb", [1, 1024], BF16)
    nmsb = SB("nmsb", [1, 1024], BF16)
    rmsrow_b = SB("rmsrow_b", [1, D], BF16)
    smallf = SB("smallf", [128, 48], F32)
    rbuf = SB("rbuf", [128, 16], BF16)
    logit_sb = SB("logit_sb", [128, 8], F32)
    gcolb = SB("gcolb", [128, 64], BF16)
    gscr = SB("gscr", [128, 16], F32)
    ident_f = SB("ident_f", [128, 128], F32)
    ident_b = SB("ident_b", [128, 128], BF16)
    ones_r = SB("ones_r", [1, 128], BF16)
    ones_c = SB("ones_c", [128, 1], BF16)
    rmst = SB("rmst", [128, ND], F32)
    router_b = SB("router_b", [128, ND * 8], BF16)
    hs_sb = SB("hs_sb", [1, 2], F32)
    gbc_sb = SB("gbc_sb", [128, 1024], BF16)
    smb_sb = SB("smb_sb", [128, 1], F32)

    AR1f = AR1[:, :].bitcast(F32)
    AR2f = AR2[:, :].bitcast(F32)
    AR3f = AR3[:, :].bitcast(F32)
    AR4f = AR4[:, :].bitcast(F32)
    growsb = rows[:, :].bitcast(BF16)
    chb = rows[0:1, 3072:4096].bitcast(BF16)

    def x1T(ot, lo=0, n=1024):
        if ot < 8:
            return AR2f[:, ot * 1024 + lo: ot * 1024 + lo + n]
        return AR3f[:, (ot - 8) * 1024 + lo: (ot - 8) * 1024 + lo + n]

    PA = PS("PA", [128, 2048], F32)
    PB = PS("PB", [128, 2048], F32)
    Q0 = PA[:, 0:1024]
    Q1 = PA[:, 1024:2048]
    Q2 = PB[:, 0:1024]
    Q3 = PB[:, 1024:2048]
    QQ = [Q0, Q1, Q2, Q3]

    sem = {n: SEM(n) for n in ["qs", "qg", "qa", "qv", "cc", "te", "ve", "se", "ge"]}
    cnt = {n: 0 for n in sem}

    def inc(inst, name, n=1):
        inst.then_inc(sem[name], n)
        cnt[name] += n
        return cnt[name]

    def wt(eng, name, val):
        if val > 0:
            eng.wait_ge(sem[name], val)

    blk_es = contextlib.ExitStack()
    block = blk_es.enter_context(nc.Block())

    def vbar(v, inst):
        val = inc(inst, "ve")
        v.wait_ge(sem["ve"], val)
        return val

    def sbar(s, inst):
        val = inc(inst, "se")
        s.wait_ge(sem["se"], val)
        return val

    def p0_g(g):
        g.memset(ones_r[:, :], 1.0)
        g.memset(ones_c[:, :], 1.0)
        g.memset(smb_sb[:, :], -6.0)
        g.memset(ident_f[:, :], 0.0)
        g.affine_select(out=ident_f[:, :], in_=ident_f[:, :],
                        compare_op=ALU.not_equal, fill=1.0, base=0,
                        pattern=[[-1, 128]], channel_multiplier=1)
        g.memset(ident_b[:, :], 0.0)
        g.affine_select(out=ident_b[:, :], in_=ident_b[:, :],
                        compare_op=ALU.not_equal, fill=1.0, base=0,
                        pattern=[[-1, 128]], channel_multiplier=1)
        inc(g.engine_nop(), "ge")
        inc(g.dma_start(out=router_b[:, :], in_=router.rearrange("p a b -> p (a b)")), "qg", 16)
        inc(g.dma_start(out=rmst[:, :], in_=rmst_in), "qg", 16)
        inc(g.dma_start(out=rmsrow_b[:, :], in_=rmsrow_in), "qg", 16)
        inc(g.dma_start(out=hs_sb[:, :], in_=hsel), "qg", 16)
    block.gpsimd(p0_g)
    ge_consts = cnt["ge"]
    qg_smalls = cnt["qg"]

    p1_load_done = [0] * NT
    p1_var_done = [0] * NT
    p1_rstd_done = [0] * NT
    p1_apply_done = [0] * NT
    p1_grp_tr = [0] * (NT * 4)
    p1_grp_evac = [0] * (NT * 4)

    def emit_p1_load(i):
        def f(eng):
            if i >= 2:
                wt(eng, "ve", p1_apply_done[i - 2])
            nm = "qs" if i % 2 == 0 else "qa"
            p1_load_done[i] = (nm, inc(
                eng.dma_start(out=AR2f[:, (i % 2) * 2048:(i % 2) * 2048 + 2048],
                              in_=x_my[i * 128:(i + 1) * 128, :]), nm, 16))
        if i % 2 == 0:
            block.sync(f)
        else:
            block.scalar(f)

    def emit_p1_work(i):
        xsl = AR2f[:, (i % 2) * 2048:(i % 2) * 2048 + 2048]
        sqx = AR3f[:, 0:2048]
        ln1tok = AR3[:, 8192 + (i % 2) * 2048: 8192 + (i % 2) * 2048 + 2048]

        def fv(v):
            wt(v, p1_load_done[i][0], p1_load_done[i][1])
            if i >= 1:
                wt(v, "se", p1_rstd_done[i - 1])
            vbar(v, v.reduce_sum(smallf[:, 0:1], xsl, axis=AX.X))
            vbar(v, v.tensor_mul(sqx, xsl, xsl))
            vbar(v, v.reduce_sum(smallf[:, 1:2], sqx, axis=AX.X))
            vbar(v, v.tensor_scalar(out=smallf[:, 2:3], in0=smallf[:, 0:1],
                                    scalar1=1.0 / D, scalar2=None, op0=ALU.mult))
            vbar(v, v.tensor_scalar(out=smallf[:, 1:2], in0=smallf[:, 1:2],
                                    scalar1=1.0 / D, scalar2=None, op0=ALU.mult))
            vbar(v, v.tensor_mul(smallf[:, 3:4], smallf[:, 2:3], smallf[:, 2:3]))
            vbar(v, v.tensor_sub(smallf[:, 3:4], smallf[:, 1:2], smallf[:, 3:4]))
            p1_var_done[i] = vbar(v, v.tensor_scalar(
                out=smallf[:, 3:4], in0=smallf[:, 3:4],
                scalar1=EPS_LN, scalar2=None, op0=ALU.add))
        block.vector(fv)

        def fs(s):
            wt(s, "ve", p1_var_done[i])
            sbar(s, s.activation(out=smallf[:, 4:5], in_=smallf[:, 3:4], func=AF.Ln))
            p1_rstd_done[i] = sbar(s, s.activation(
                out=smallf[:, 4:5], in_=smallf[:, 4:5], func=AF.Exp, scale=-0.5))
        block.scalar(fs)

        def fv2(v):
            wt(v, "se", p1_rstd_done[i])
            vbar(v, v.tensor_mul(smallf[:, 5:6], smallf[:, 2:3], smallf[:, 4:5]))
            vbar(v, v.tensor_scalar(out=smallf[:, 5:6], in0=smallf[:, 5:6],
                                    scalar1=-1.0, scalar2=None, op0=ALU.mult))
            p1_apply_done[i] = vbar(v, v.tensor_scalar(
                out=ln1tok, in0=xsl, scalar1=smallf[:, 4:5],
                scalar2=smallf[:, 5:6], op0=ALU.mult, op1=ALU.add))
        block.vector(fv2)

        for grp in range(4):
            def ft(t, grp=grp):
                if grp == 0:
                    wt(t, "ve", p1_apply_done[i])
                if i == 0 and grp == 0:
                    wt(t, "ge", ge_consts)
                psv = (Q2 if grp % 2 == 0 else Q3)[:, 0:512].bitcast(BF16)
                gi = i * 4 + grp
                if gi >= 2:
                    wt(t, "se", p1_grp_evac[gi - 2])
                for j in range(4):
                    dt = grp * 4 + j
                    last = t.transpose(psv[:, j * 128:(j + 1) * 128],
                                       ln1tok[:, dt * 128:(dt + 1) * 128], ident_b[:, :])
                p1_grp_tr[gi] = inc(last, "te")
            block.tensor(ft)

            def fs2(s, grp=grp):
                dstv = AR1[:, :].rearrange("p (a b) -> p a b", a=16)
                psv = (Q2 if grp % 2 == 0 else Q3)[:, 0:512].bitcast(BF16)
                gi = i * 4 + grp
                wt(s, "te", p1_grp_tr[gi])
                srcv = psv[:, 0:512].rearrange("p (a b) -> p a b", a=4)
                p1_grp_evac[gi] = inc(
                    s.activation(out=dstv[:, grp * 4:(grp + 1) * 4, i * 128:(i + 1) * 128],
                                 in_=srcv[:, :, :], func=AF.Copy), "se")
            block.scalar(fs2)

    emit_p1_load(0)
    emit_p1_load(1)
    for i in range(NT):
        emit_p1_work(i)
        if i + 2 < NT:
            emit_p1_load(i + 2)
    ln1_done_se = p1_grp_evac[NT * 4 - 1]

    if "ln1T" in dbg_out:
        def fdbg(sy):
            wt(sy, "se", ln1_done_se)
            inc(sy.dma_start(out=dbg_out["ln1T"], in_=AR1[:, :]), "qs", 16)
        block.sync(fdbg)

    if upto <= 1:
        def f_final(sy):
            wt(sy, "qs", cnt["qs"])
        block.sync(f_final)
        blk_es.close()
        es.close()
        return nc, dbg_out

    order = list(range(16, 32)) + list(range(0, 16)) + list(range(32, 48))
    NS = 3
    p2_ld_done = [0] * NCT
    p2_mm_done = [0] * NCT
    p2_ev_done = [0] * NCT
    kin_dma = [0] * H
    qT_ap = AR2
    vT_ap = AR4

    def emit_p2_load(idx):
        ct = order[idx]
        def f(g):
            if idx >= NS:
                wt(g, "te", p2_mm_done[idx - NS])
            if idx < 2:
                wt(g, "ve", p1_apply_done[NT - 1])
            sl = S_r[:, (idx % NS) * 2048:(idx % NS) * 2048 + 2048]
            p2_ld_done[idx] = ("qg", inc(
                g.dma_start(out=sl, in_=wqkv[ct].rearrange("p a b -> p (a b)")),
                "qg", 16))
        block.gpsimd(f)

    def emit_p2_mm(idx):
        ps = Q2 if idx % 2 == 0 else Q3
        sl = S_r[:, (idx % NS) * 2048:(idx % NS) * 2048 + 2048]
        def f(t):
            wt(t, p2_ld_done[idx][0], p2_ld_done[idx][1])
            if idx == 0:
                wt(t, "se", ln1_done_se)
            if idx >= 2:
                wt(t, "se", p2_ev_done[idx - 2])
            for tc in range(2):
                for dt in range(ND):
                    last = t.matmul(ps[:, tc * 512:(tc + 1) * 512],
                                    sl[:, dt * 128:(dt + 1) * 128],
                                    AR1[:, dt * 1024 + tc * 512: dt * 1024 + tc * 512 + 512],
                                    start=(dt == 0), stop=(dt == ND - 1))
            p2_mm_done[idx] = inc(last, "te")
        block.tensor(f)

    def emit_p2_evac(idx):
        ct = order[idx]
        ps = Q2 if idx % 2 == 0 else Q3
        def f(s):
            wt(s, "te", p2_mm_done[idx])
            if ct < 16:
                dst = qT_ap[:, ct * 1024:(ct + 1) * 1024]
                scale = ISQ
            elif ct < 32:
                if idx >= 2:
                    wt(s, "qs", kin_dma[order[idx - 2] - 16])
                dst = kt_r[:, (idx % 2) * 1024:(idx % 2) * 1024 + 1024]
                scale = 1.0
            else:
                dst = vT_ap[:, (ct - 32) * 1024:(ct - 32 + 1) * 1024]
                scale = 1.0
            p2_ev_done[idx] = inc(
                s.activation(out=dst, in_=ps[:, :], func=AF.Copy, scale=scale), "se")
        block.scalar(f)
        if 16 <= ct < 32:
            h = ct - 16
            def f2(sy):
                wt(sy, "se", p2_ev_done[idx])
                kin_dma[h] = inc(
                    sy.dma_start(out=kin[h * 128:(h + 1) * 128, :],
                                 in_=kt_r[:, (idx % 2) * 1024:(idx % 2) * 1024 + 1024]),
                    "qs", 16)
            block.sync(f2)

    emit_p2_load(0)
    emit_p2_load(1)
    emit_p2_load(2)
    for idx in range(NCT):
        emit_p2_mm(idx)
        emit_p2_evac(idx)
        if idx + NS < NCT:
            emit_p2_load(idx + NS)

    def f_ag(g):
        wt(g, "qs", max(kin_dma))
        inc(g.collective_compute("AllGather", ALU.bypass, replica_groups=PAIRS,
                                 ins=[kin.opt()], outs=[kag.opt()]), "cc")
    block.gpsimd(f_ag)
    cc_ag = cnt["cc"]
    p2_all_ev = cnt["se"]

    for nm, buf in [("qT", qT_ap), ("vT", vT_ap)]:
        if nm in dbg_out:
            def fdbg(sy, buf=buf, nm=nm):
                wt(sy, "se", p2_all_ev)
                inc(sy.dma_start(out=dbg_out[nm], in_=buf[:, :]), "qs", 16)
            block.sync(fdbg)
    if "kag" in dbg_out:
        def fdbg(sy):
            wt(sy, "cc", cc_ag)
            inc(sy.dma_start(out=dbg_out["kag"], in_=kag), "qs", 16)
        block.sync(fdbg)

    wo_ld = [0] * ND
    womm_done = [0] * ND

    def emit_wo_load(ot):
        def f(g):
            idx = NCT + ot
            wt(g, "te", p2_mm_done[NCT - NS + ot] if ot < NS else womm_done[ot - NS])
            sl = S_r[:, (idx % NS) * 2048:(idx % NS) * 2048 + 2048]
            wo_ld[ot] = inc(
                g.dma_start(out=sl, in_=wo[ot].rearrange("p a b -> p (a b)")), "qg", 16)
        block.gpsimd(f)

    emit_wo_load(0)
    emit_wo_load(1)

    if upto <= 2:
        def f_final(sy):
            wt(sy, "qs", cnt["qs"])
        block.sync(f_final)
        blk_es.close()
        es.close()
        return nc, dbg_out

    SMB = -6.0
    p2_q_ev = p2_ev_done[31]
    kf_done = [0] * H
    sc2 = [0] * (H * 16)
    exp2 = [0] * (H * 16)
    rr2 = [[0] * 8 for _ in range(H)]
    mv2 = [[0, 0] for _ in range(H)]
    cev2 = [[0, 0] for _ in range(H)]
    cin_done = [0] * H
    SLOTS = [PA[:, 0:1024], PA[:, 1024:2048], PB[:, 0:1024]]
    CPc = [PB[0:1, 1024:1536], PB[0:1, 1536:2048]]

    def E_ap(h, qt):
        buf = AR3 if h % 2 == 0 else AR1
        return buf[:, qt * 2048:(qt + 1) * 2048]

    def emit_kf_load(h):
        def f(sy):
            wt(sy, "cc", cc_ag)
            if h >= 2:
                wt(sy, "te", sc2[(h - 2) * 16 + 15])
            sl = kf_r[:, (h % 2) * 2048:(h % 2) * 2048 + 2048]
            inc(sy.dma_start(out=sl[:, 0:1024],
                             in_=kag[h * 128:(h + 1) * 128, :]), "qs", 16)
            kf_done[h] = inc(
                sy.dma_start(out=sl[:, 1024:2048],
                             in_=kag[2048 + h * 128: 2048 + (h + 1) * 128, :]),
                "qs", 16)
        block.sync(f)

    def emit_score_unit(h, qt, half):
        u = h * 16 + qt * 2 + half
        slot = SLOTS[u % 3]
        kfb = (h % 2) * 2048 + half * 1024

        def ft(t):
            if u == 0:
                wt(t, "se", p2_q_ev)
            if qt == 0 and half == 0:
                wt(t, "qs", kf_done[h])
            if u == 2:
                wt(t, "se", p2_ev_done[NCT - 1])
            if u >= 3:
                wt(t, "se", exp2[u - 3])
            for c in range(2):
                last = t.matmul(
                    slot[:, c * 512:(c + 1) * 512],
                    qT_ap[:, h * 1024 + qt * 128: h * 1024 + (qt + 1) * 128],
                    kf_r[:, kfb + c * 512: kfb + (c + 1) * 512],
                    start=True, stop=True)
            sc2[u] = inc(last, "te")
        block.tensor(ft)

        def fs(s):
            wt(s, "te", sc2[u])
            if qt == 0 and half == 0:
                if h == 1:
                    wt(s, "te", p2_mm_done[NCT - 1])
                if h >= 2:
                    wt(s, "te", mv2[h - 2][1])
                    wt(s, "ve", rr2[h - 2][7])
            acc = smallf[:, 6 + (h % 2) * 16 + qt * 2 + half:
                         7 + (h % 2) * 16 + qt * 2 + half]
            exp2[u] = inc(
                s.activation(out=E_ap(h, qt)[:, half * 1024:half * 1024 + 1024],
                             in_=slot[:, :], func=AF.Exp, bias=smb_sb[:, 0:1],
                             accum_out=acc), "se")
        block.scalar(fs)

    def emit_rr(h, qt):
        def fv(v):
            wt(v, "se", exp2[h * 16 + qt * 2 + 1])
            base = 6 + (h % 2) * 16 + qt * 2
            vbar(v, v.tensor_add(smallf[:, 38:39], smallf[:, base:base + 1],
                                 smallf[:, base + 1:base + 2]))
            vbar(v, v.reciprocal(smallf[:, 39:40], smallf[:, 38:39]))
            rr2[h][qt] = vbar(v, v.tensor_copy(
                rbuf[:, (h % 2) * 8 + qt:(h % 2) * 8 + qt + 1],
                smallf[:, 39:40]))
        block.vector(fv)

    def emit_mv(h, half):
        def ft(t):
            if half == 0:
                wt(t, "ve", rr2[h][7])
                if h == 0:
                    wt(t, "se", p2_ev_done[NCT - 1])
                if h >= 1:
                    wt(t, "ve", cev2[h - 1][1])
            else:
                wt(t, "ve", cev2[h][0])
            for qt in range(8):
                for c in range(2):
                    last = t.matmul(
                        CPc[c],
                        rbuf[:, (h % 2) * 8 + qt:(h % 2) * 8 + qt + 1],
                        E_ap(h, qt)[:, half * 1024 + c * 512:
                                     half * 1024 + (c + 1) * 512],
                        start=(qt == 0), stop=(qt == 7))
            mv2[h][half] = inc(last, "te")
        block.tensor(ft)

    def emit_cev(h, half):
        def fv(v):
            wt(v, "te", mv2[h][half])
            if half == 0 and h >= 1:
                wt(v, "qs", cin_done[h - 1])
            v.tensor_copy(rows[0:1, half * 1024:half * 1024 + 512], CPc[0])
            cev2[h][half] = inc(
                v.tensor_copy(rows[0:1, half * 1024 + 512:half * 1024 + 1024],
                              CPc[1]), "ve")
        block.vector(fv)

    def emit_cin(h):
        def fy(sy):
            wt(sy, "ve", cev2[h][1])
            cin_done[h] = inc(
                sy.dma_start(out=cin[h:h + 1, :], in_=rows[0:1, 0:2048]), "qs", 16)
        block.sync(fy)

    emit_kf_load(0)
    emit_kf_load(1)
    for h in range(H):
        for qt in range(8):
            for half in range(2):
                emit_score_unit(h, qt, half)
            emit_rr(h, qt)
            if h > 0:
                if qt == 2:
                    emit_mv(h - 1, 0)
                    emit_cev(h - 1, 0)
                elif qt == 4:
                    emit_mv(h - 1, 1)
                    emit_cev(h - 1, 1)
                    emit_cin(h - 1)
        if h + 2 < H:
            emit_kf_load(h + 2)
    emit_mv(H - 1, 0)
    emit_cev(H - 1, 0)
    emit_mv(H - 1, 1)
    emit_cev(H - 1, 1)
    emit_cin(H - 1)

    if "cin" in dbg_out:
        def fdbg(sy):
            wt(sy, "qs", cin_done[H - 1])
            inc(sy.dma_start(out=dbg_out["cin"], in_=cin), "qs", 16)
        block.sync(fdbg)

    if upto <= 2.7:
        def f_final(sy):
            wt(sy, "qs", cnt["qs"])
        block.sync(f_final)
        blk_es.close()
        es.close()
        return nc, dbg_out

    def f_ar(g):
        wt(g, "qs", cin_done[H - 1])
        inc(g.collective_compute("AllReduce", ALU.add, replica_groups=PAIRS,
                                 ins=[cin.opt()], outs=[car.opt()]), "cc")
    block.gpsimd(f_ar)
    cc_ar = cnt["cc"]

    if "car" in dbg_out:
        def fdbg(sy):
            wt(sy, "cc", cc_ar)
            inc(sy.dma_start(out=dbg_out["car"], in_=car), "qs", 16)
        block.sync(fdbg)

    if upto <= 3:
        def f_final(sy):
            wt(sy, "qs", cnt["qs"])
        block.sync(f_final)
        blk_es.close()
        es.close()
        return nc, dbg_out

    car_ld = [0] * H
    cmy_done = [0] * H
    bc_done = [0] * H
    vmul_done = [0] * H

    def crow2(h):
        return rows[0:1, 0:2048] if h % 2 == 0 else AR2f[0:1, 4096:6144]

    def emit_v_load(h):
        def f(sy):
            wt(sy, "cc", cc_ar)
            if h >= 2:
                wt(sy, "ve", cmy_done[h - 2])
            car_ld[h] = inc(
                sy.dma_start(out=crow2(h), in_=car[h:h + 1, :]), "qs", 16)
        block.sync(f)

    def emit_v_cmy(h):
        def fv(v):
            wt(v, "qs", car_ld[h])
            if h >= 2:
                wt(v, "te", bc_done[h - 2])
            cr = crow2(h)
            vbar(v, v.tensor_scalar(out=rows[0:1, 2048:3072], in0=cr[0:1, 0:1024],
                                    scalar1=hs_sb[0:1, 0:1], scalar2=None, op0=ALU.mult))
            vbar(v, v.scalar_tensor_tensor(out=rows[0:1, 2048:3072],
                                           in0=cr[0:1, 1024:2048],
                                           scalar=hs_sb[0:1, 1:2],
                                           in1=rows[0:1, 2048:3072],
                                           op0=ALU.mult, op1=ALU.add))
            cmy_done[h] = vbar(v, v.tensor_copy(
                chb[0:1, (h % 2) * 1024:(h % 2) * 1024 + 1024], rows[0:1, 2048:3072]))
        block.vector(fv)

    def emit_v_bc(h):
        def ft(t):
            wt(t, "ve", cmy_done[h])
            if h == 0:
                wt(t, "ve", cev2[H - 1][1])
            if h >= 2:
                wt(t, "ve", vmul_done[h - 2])
            qb = Q0 if h % 2 == 0 else Q1
            base = (h % 2) * 1024
            t.matmul(qb[:, 0:512], ones_r[0:1, :],
                     chb[0:1, base:base + 512], start=True, stop=True)
            last = t.matmul(qb[:, 512:1024], ones_r[0:1, :],
                            chb[0:1, base + 512:base + 1024], start=True, stop=True)
            bc_done[h] = inc(last, "te")
        block.tensor(ft)

    def emit_v_mul(h):
        def fv2(v):
            wt(v, "te", bc_done[h])
            qb = Q0 if h % 2 == 0 else Q1
            vmul_done[h] = inc(
                v.tensor_tensor(vT_ap[:, h * 1024:(h + 1) * 1024],
                                vT_ap[:, h * 1024:(h + 1) * 1024],
                                qb[:, :], op=ALU.mult), "ve")
        block.vector(fv2)

    emit_v_load(0)
    emit_v_cmy(0)
    for h in range(H):
        if h + 1 < H:
            emit_v_load(h + 1)
        emit_v_bc(h)
        if h + 1 < H:
            emit_v_cmy(h + 1)
        emit_v_mul(h)
    values_done = vmul_done[H - 1]

    if "values" in dbg_out:
        def fdbg(sy):
            wt(sy, "ve", values_done)
            inc(sy.dma_start(out=dbg_out["values"], in_=vT_ap[:, :]), "qs", 16)
        block.sync(fdbg)

    if upto <= 4:
        def f_final(sy):
            wt(sy, "qs", cnt["qs"])
        block.sync(f_final)
        blk_es.close()
        es.close()
        return nc, dbg_out

    xc_ld = [0] * ND
    tr_done = [0] * ND
    xev_done = [0] * ND
    add_done = [0] * ND

    def emit_xc_load(ot):
        def f(sy):
            if ot >= 2:
                wt(sy, "te", tr_done[ot - 2])
            src = x_my[:, ot * 128:(ot + 1) * 128].rearrange("(i p) j -> p i j", p=128)
            xc_ld[ot] = inc(
                sy.dma_start(out=xc_r[:, (ot % 2) * 1024:(ot % 2) * 1024 + 1024],
                             in_=src), "qs", 16)
        block.sync(f)

    def emit_p4(ot):
        idx = NCT + ot
        sl = S_r[:, (idx % NS) * 2048:(idx % NS) * 2048 + 2048]
        xcs = xc_r[:, (ot % 2) * 1024:(ot % 2) * 1024 + 1024]

        trQ = Q0 if ot % 2 == 0 else Q2
        woQ = Q1 if ot % 2 == 0 else Q3

        def ft(t):
            wt(t, "qs", xc_ld[ot])
            wt(t, "qg", wo_ld[ot])
            if ot <= 1:
                wt(t, "ve", values_done)
            if ot >= 2:
                wt(t, "ve", xev_done[ot - 2])
            for j in range(8):
                last = t.transpose(trQ[:, j * 128:(j + 1) * 128],
                                   xcs[:, j * 128:(j + 1) * 128], ident_f[:, :])
            tr_done[ot] = inc(last, "te")
            if ot >= 2:
                wt(t, "ve", add_done[ot - 2])
            for tc in range(2):
                for ft_ in range(ND):
                    last = t.matmul(woQ[:, tc * 512:(tc + 1) * 512],
                                    sl[:, ft_ * 128:(ft_ + 1) * 128],
                                    vT_ap[:, ft_ * 1024 + tc * 512: ft_ * 1024 + tc * 512 + 512],
                                    start=(ft_ == 0), stop=(ft_ == ND - 1))
            womm_done[ot] = inc(last, "te")
        block.tensor(ft)

        def fv(v):
            wt(v, "te", tr_done[ot])
            xev_done[ot] = inc(v.tensor_copy(xrT[:, :], trQ[:, :]), "ve")
            wt(v, "te", womm_done[ot])
            v.tensor_add(x1T(ot, 0, 512), woQ[:, 0:512], xrT[:, 0:512])
            add_done[ot] = inc(
                v.tensor_add(x1T(ot, 512, 512), woQ[:, 512:1024], xrT[:, 512:1024]), "ve")
        block.vector(fv)

    emit_xc_load(0)
    emit_xc_load(1)
    for ot in range(ND):
        emit_p4(ot)
        if ot + 2 < ND:
            emit_wo_load(ot + 2)
            emit_xc_load(ot + 2)
    x1_done = add_done[ND - 1]

    if "x1T" in dbg_out:
        def fdbg(sy):
            wt(sy, "ve", x1_done)
            inc(sy.dma_start(out=dbg_out["x1T"][:, 0:8192], in_=AR2f[:, :]), "qs", 16)
            inc(sy.dma_start(out=dbg_out["x1T"][:, 8192:16384], in_=AR3f[:, :]), "qs", 16)
        block.sync(fdbg)

    if upto <= 5:
        def f_final(sy):
            wt(sy, "qs", cnt["qs"])
        block.sync(f_final)
        blk_es.close()
        es.close()
        return nc, dbg_out

    cast_done = [0] * ND
    sq_done = [0] * ND
    stat_mm = [0] * ND

    for dt in range(ND):
        def fv(v, dt=dt):
            wt(v, "ve", add_done[dt])
            if dt == 0:
                wt(v, "te", womm_done[ND - 1])
            cast_done[dt] = inc(
                v.tensor_copy(AR4[:, dt * 1024:(dt + 1) * 1024], x1T(dt)), "ve")
            if dt >= 2:
                wt(v, "te", stat_mm[dt - 2])
            if dt == 0:
                wt(v, "se", exp2[H * 16 - 1])
                wt(v, "te", mv2[H - 1][1])
            sq_done[dt] = inc(v.tensor_mul(
                AR1[:, (dt % 2) * 1024:(dt % 2) * 1024 + 1024],
                x1T(dt), x1T(dt)), "ve")
        block.vector(fv)

        def ft(t, dt=dt):
            wt(t, "ve", sq_done[dt])
            if dt == 0:
                wt(t, "ve", cev2[H - 1][1])
                wt(t, "ve", add_done[ND - 1])
            sqs = AR1[:, (dt % 2) * 1024:(dt % 2) * 1024 + 1024]
            x1b = AR4[:, dt * 1024:(dt + 1) * 1024]
            for ch in range(2):
                t.matmul(Q0[0:1, ch * 512:(ch + 1) * 512], ones_c[:, 0:1],
                         x1b[:, ch * 512:(ch + 1) * 512],
                         start=(dt == 0), stop=(dt == ND - 1))
                last = t.matmul(Q1[0:1, ch * 512:(ch + 1) * 512], ones_c[:, 0:1],
                                sqs[:, ch * 512:(ch + 1) * 512],
                                start=(dt == 0), stop=(dt == ND - 1))
            stat_mm[dt] = inc(last, "te")
        block.tensor(ft)
    stats_mm_done = stat_mm[ND - 1]

    if "x1b" in dbg_out:
        def fdbg(sy):
            wt(sy, "ve", cast_done[ND - 1])
            inc(sy.dma_start(out=dbg_out["x1b"], in_=AR4[:, :]), "qs", 16)
        block.sync(fdbg)

    if upto <= 6:
        def f_final(sy):
            wt(sy, "qs", cnt["qs"])
        block.sync(f_final)
        blk_es.close()
        es.close()
        return nc, dbg_out

    A_ = rows[0:1, 0:1024]
    Bs_ = rows[0:1, 1024:2048]
    C_ = rows[0:1, 2048:3072]
    T_ = rows[0:1, 3072:4096]

    def f_statev(s):
        wt(s, "te", stats_mm_done)
        s.activation(out=A_[0:1, 0:512], in_=Q0[0:1, 0:512], func=AF.Copy)
        s.activation(out=A_[0:1, 512:1024], in_=Q0[0:1, 512:1024], func=AF.Copy)
        s.activation(out=Bs_[0:1, 0:512], in_=Q1[0:1, 0:512], func=AF.Copy)
        sbar(s, s.activation(out=Bs_[0:1, 512:1024], in_=Q1[0:1, 512:1024],
                             func=AF.Copy))
        stat_ev = cnt["se"]
        return stat_ev
    block.scalar(f_statev)
    stat_ev = cnt["se"]

    def f_rows1(v):
        wt(v, "se", stat_ev)
        vbar(v, v.tensor_scalar(out=C_, in0=A_, scalar1=1.0 / D, scalar2=None,
                                op0=ALU.mult))
        vbar(v, v.tensor_scalar(out=A_, in0=Bs_, scalar1=1.0 / D, scalar2=None,
                                op0=ALU.mult))
        vbar(v, v.tensor_mul(T_, C_, C_))
        vbar(v, v.tensor_sub(A_, A_, T_))
        vbar(v, v.tensor_scalar(out=Bs_, in0=A_, scalar1=EPS_LN, scalar2=None,
                                op0=ALU.add))
    block.vector(f_rows1)
    rows1_ve = cnt["ve"]

    def f_rows2(s):
        wt(s, "ve", rows1_ve)
        sbar(s, s.activation(out=T_, in_=Bs_, func=AF.Ln))
        sbar(s, s.activation(out=Bs_, in_=T_, func=AF.Exp, scale=-0.5))
    block.scalar(f_rows2)
    rows2_se = cnt["se"]

    def f_rows3(v):
        wt(v, "se", rows2_se)
        vbar(v, v.tensor_mul(T_, Bs_, Bs_))
        vbar(v, v.tensor_mul(T_, A_, T_))
        vbar(v, v.tensor_scalar(out=T_, in0=T_, scalar1=EPS_RMS, scalar2=None,
                                op0=ALU.add))
    block.vector(f_rows3)
    rows3_ve = cnt["ve"]

    def f_rows4(s):
        wt(s, "ve", rows3_ve)
        sbar(s, s.activation(out=A_, in_=T_, func=AF.Ln))
        sbar(s, s.activation(out=T_, in_=A_, func=AF.Exp, scale=-0.5))
    block.scalar(f_rows4)
    rows4_se = cnt["se"]

    def f_rows5(v):
        wt(v, "se", rows4_se)
        vbar(v, v.tensor_mul(A_, Bs_, T_))
        vbar(v, v.tensor_copy(sigb[0:1, :], A_))
        vbar(v, v.tensor_mul(T_, C_, A_))
        vbar(v, v.tensor_scalar(out=T_, in0=T_, scalar1=-1.0, scalar2=None,
                                op0=ALU.mult))
        vbar(v, v.tensor_copy(nmsb[0:1, :], T_))
    block.vector(f_rows5)
    rows5_ve = cnt["ve"]

    if "stats" in dbg_out:
        def fdbg(sy):
            wt(sy, "ve", rows5_ve)
            inc(sy.dma_start(out=dbg_out["stats"], in_=rows[0:1, :]), "qs", 16)
        block.sync(fdbg)

    if upto <= 7:
        def f_final(sy):
            wt(sy, "qs", cnt["qs"])
        block.sync(f_final)
        blk_es.close()
        es.close()
        return nc, dbg_out

    def f_sbc(t):
        wt(t, "ve", rows5_ve)
        wt(t, "ve", add_done[ND - 1])
        wt(t, "qg", qg_smalls)
        t.matmul(Q3[:, 0:512], ones_r[0:1, :], sigb[0:1, 0:512], start=True, stop=True)
        last = t.matmul(Q3[:, 512:1024], ones_r[0:1, :], sigb[0:1, 512:1024],
                        start=True, stop=True)
        inc(last, "te")
    block.tensor(f_sbc)
    sbc_te = cnt["te"]

    outer_done = [0] * ND
    h2_done = [0] * ND
    for dt in range(ND):
        def ft(t, dt=dt):
            if dt == 0:
                wt(t, "se", stat_ev)
            if dt >= 2:
                wt(t, "ve", h2_done[dt - 2])
            half = Q0 if dt % 2 == 0 else Q1
            t.matmul(half[:, 0:512], rmsrow_b[0:1, dt * 128:(dt + 1) * 128],
                     nmsb[0:1, 0:512], start=True, stop=True)
            last = t.matmul(half[:, 512:1024], rmsrow_b[0:1, dt * 128:(dt + 1) * 128],
                            nmsb[0:1, 512:1024], start=True, stop=True)
            outer_done[dt] = inc(last, "te")
        block.tensor(ft)

        def fv(v, dt=dt):
            wt(v, "te", sbc_te)
            wt(v, "te", outer_done[dt])
            if dt < 2:
                wt(v, "te", stat_mm[ND - 1])
            h2sl = AR1[:, dt * 1024:(dt + 1) * 1024]
            half = Q0 if dt % 2 == 0 else Q1
            vbar(v, v.scalar_tensor_tensor(out=h2sl, in0=x1T(dt),
                                           scalar=rmst[:, dt:dt + 1], in1=Q3[:, :],
                                           op0=ALU.mult, op1=ALU.mult))
            h2_done[dt] = inc(
                v.tensor_tensor(h2sl, h2sl, half[:, :], op=ALU.add), "ve")
        block.vector(fv)
    h2_all = h2_done[ND - 1]

    if "h2T" in dbg_out:
        def fdbg(sy):
            wt(sy, "ve", h2_all)
            inc(sy.dma_start(out=dbg_out["h2T"], in_=AR1[:, :]), "qs", 16)
        block.sync(fdbg)

    if upto <= 8:
        def f_final(sy):
            wt(sy, "qs", cnt["qs"])
        block.sync(f_final)
        blk_es.close()
        es.close()
        return nc, dbg_out

    lg_done = [0] * NT
    lgev_done = [0] * NT
    gate_done = [0] * NT

    for i in range(NT):
        def ft(t, i=i):
            wt(t, "ve", h2_all)
            if i >= 1:
                wt(t, "ve", lgev_done[i - 1])
            for dt in range(ND):
                last = t.matmul(Q0[:, 0:8],
                                AR1[:, dt * 1024 + i * 128: dt * 1024 + (i + 1) * 128],
                                router_b[:, dt * 8:(dt + 1) * 8],
                                start=(dt == 0), stop=(dt == ND - 1))
            lg_done[i] = inc(last, "te")
        block.tensor(ft)

        def fv(v, i=i):
            wt(v, "te", lg_done[i])
            lgev_done[i] = vbar(v, v.tensor_copy(logit_sb[:, :], Q0[:, 0:8]))
            vbar(v, v.reduce_max(gscr[:, 0:1], logit_sb[:, :], axis=AX.X))
            vbar(v, v.tensor_scalar(out=gscr[:, 8:16], in0=logit_sb[:, :],
                                    scalar1=gscr[:, 0:1], scalar2=None,
                                    op0=ALU.is_equal))
            vbar(v, v.tensor_scalar(out=gscr[:, 8:16], in0=gscr[:, 8:16],
                                    scalar1=2e30, scalar2=None, op0=ALU.mult))
            vbar(v, v.tensor_sub(gscr[:, 8:16], logit_sb[:, :], gscr[:, 8:16]))
            vbar(v, v.reduce_max(gscr[:, 1:2], gscr[:, 8:16], axis=AX.X))
            vbar(v, v.tensor_scalar(out=gscr[:, 2:3], in0=gscr[:, 0:1],
                                    scalar1=-1.0, scalar2=None, op0=ALU.mult))
        block.vector(fv)
        g_m_ve = cnt["ve"]

        def fs(s, i=i, g_m_ve=g_m_ve):
            wt(s, "ve", g_m_ve)
            s.activation(out=gscr[:, 3:4], in_=gscr[:, 1:2], func=AF.Exp,
                         bias=gscr[:, 2:3]).then_inc(sem["se"], 1)
        block.scalar(fs)
        cnt["se"] += 1
        g_e2_se = cnt["se"]

        def fv2(v, i=i, g_e2_se=g_e2_se):
            wt(v, "se", g_e2_se)
            vbar(v, v.tensor_scalar(out=gscr[:, 4:5], in0=gscr[:, 3:4],
                                    scalar1=1.0, scalar2=None, op0=ALU.add))
            vbar(v, v.reciprocal(gscr[:, 5:6], gscr[:, 4:5]))
            vbar(v, v.tensor_scalar(out=gcolb[:, i * 8:(i + 1) * 8],
                                    in0=logit_sb[:, :], scalar1=gscr[:, 0:1],
                                    scalar2=gscr[:, 5:6], op0=ALU.is_equal,
                                    op1=ALU.mult))
            vbar(v, v.tensor_mul(gscr[:, 3:4], gscr[:, 3:4], gscr[:, 5:6]))
            vbar(v, v.tensor_scalar(out=gscr[:, 8:16], in0=logit_sb[:, :],
                                    scalar1=gscr[:, 1:2], scalar2=gscr[:, 3:4],
                                    op0=ALU.is_equal, op1=ALU.mult))
            gate_done[i] = vbar(v, v.tensor_add(gcolb[:, i * 8:(i + 1) * 8],
                                                gcolb[:, i * 8:(i + 1) * 8],
                                                gscr[:, 8:16]))
        block.vector(fv2)
    gates_all = gate_done[NT - 1]

    if "gates" in dbg_out:
        def fdbg(sy):
            wt(sy, "ve", gates_all)
            inc(sy.dma_start(out=dbg_out["gates"], in_=gcolb[:, :]), "qs", 16)
        block.sync(fdbg)

    grow_ev = [0] * 8
    for e in range(8):
        def ft(t, e=e):
            wt(t, "ve", gates_all)
            if e == 0:
                wt(t, "ve", h2_done[ND - 1])
            else:
                wt(t, "se", grow_ev[e - 1])
            for i in range(NT):
                last = t.matmul(Q1[0:1, i * 128:(i + 1) * 128],
                                gcolb[:, i * 8 + e: i * 8 + e + 1], ident_b[:, :],
                                start=True, stop=True)
            inc(last, "te")
        block.tensor(ft)
        g_row_te = cnt["te"]

        def fs(s, e=e, g_row_te=g_row_te):
            wt(s, "te", g_row_te)
            grow_ev[e] = inc(
                s.activation(out=growsb[0:1, e * 1024:(e + 1) * 1024],
                             in_=Q1[0:1, 0:1024], func=AF.Copy), "se")
        block.scalar(fs)
    grows_all = grow_ev[7]

    if "grows" in dbg_out:
        def fdbg(sy):
            wt(sy, "se", grows_all)
            inc(sy.dma_start(out=dbg_out["grows"], in_=growsb[0:1, 0:8192]), "qs", 16)
        block.sync(fdbg)

    if upto <= 9:
        def f_final(sy):
            wt(sy, "qs", cnt["qs"])
        block.sync(f_final)
        blk_es.close()
        es.close()
        return nc, dbg_out

    ngu = NE * NF * 2
    gu_ld = [0] * ngu
    gu_mm = [0] * ngu
    guev = [0] * (NE * NF)
    fold_done = [0] * (NE * NF)
    bcg_done = [0] * NE
    wd_ld = [0] * (NE * ND)
    dn_mm = [0] * (NE * ND)
    dn_add = [0] * (NE * ND)

    def emit_gu_load(j):
        e, r = divmod(j, 2 * NF)
        ft_, gu = divmod(r, 2)
        src = (wg_in if gu == 0 else wu_in)[e, ft_]
        def f(g):
            if j < NS:
                wt(g, "te", womm_done[ND - 1])
            else:
                wt(g, "te", gu_mm[j - NS])
            sl = S_r[:, (j % NS) * 2048:(j % NS) * 2048 + 2048]
            gu_ld[j] = ("qg", inc(
                g.dma_start(out=sl, in_=src.rearrange("p a b -> p (a b)")), "qg", 16))
        block.gpsimd(f)

    def emit_wd_load(e, dt):
        j = e * ND + dt
        def f(g):
            if j >= 4:
                wt(g, "te", dn_mm[j - 4])
            sl = wd_r[:, (j % 4) * 1024:(j % 4) * 1024 + 1024]
            wd_ld[j] = ("qg", inc(
                g.dma_start(out=sl, in_=wd_in[e, dt].rearrange("p a b -> p (a b)")),
                "qg", 16))
        block.gpsimd(f)

    gbc_done = [0] * NE

    def emit_bcg(e):
        def ft(t):
            wt(t, "se", grows_all)
            if e >= 1:
                wt(t, "ve", dn_add[(e - 1) * ND + ND - 1])
            t.matmul(Q2[:, 0:512], ones_r[0:1, :],
                     growsb[0:1, e * 1024:e * 1024 + 512], start=True, stop=True)
            last = t.matmul(Q2[:, 512:1024], ones_r[0:1, :],
                            growsb[0:1, e * 1024 + 512:(e + 1) * 1024],
                            start=True, stop=True)
            bcg_done[e] = inc(last, "te")
        block.tensor(ft)

        def fvc(v):
            wt(v, "te", bcg_done[e])
            gbc_done[e] = inc(v.tensor_copy(gbc_sb[:, :], Q2[:, :]), "ve")
        block.vector(fvc)

    def emit_gu(e, ft_):
        jg = (e * NF + ft_) * 2
        ju = jg + 1
        k = e * NF + ft_

        gQ = Q0 if ft_ % 2 == 0 else Q2
        uQ = Q1 if ft_ % 2 == 0 else Q3
        sgo = 8192 + (ft_ % 2) * 2048
        uo = sgo + 1024

        def ft(t):
            wt(t, gu_ld[jg][0], gu_ld[jg][1])
            wt(t, gu_ld[ju][0], gu_ld[ju][1])
            if ft_ == 0 and e == 0:
                wt(t, "se", grow_ev[7])
                wt(t, "ve", h2_all)
            if ft_ == 1 and e >= 1:
                wt(t, "ve", dn_add[(e - 1) * ND + ND - 1])
            if ft_ == 1 and e < 8:
                wt(t, "ve", gbc_done[e])
            if k >= 2:
                wt(t, "se", guev[k - 2])
            slg = S_r[:, (jg % NS) * 2048:(jg % NS) * 2048 + 2048]
            slu = S_r[:, (ju % NS) * 2048:(ju % NS) * 2048 + 2048]
            for tc in range(2):
                for dt in range(ND):
                    last = t.matmul(gQ[:, tc * 512:(tc + 1) * 512],
                                    slg[:, dt * 128:(dt + 1) * 128],
                                    AR1[:, dt * 1024 + tc * 512: dt * 1024 + tc * 512 + 512],
                                    start=(dt == 0), stop=(dt == ND - 1))
            gu_mm[jg] = inc(last, "te")
            for tc in range(2):
                for dt in range(ND):
                    last = t.matmul(uQ[:, tc * 512:(tc + 1) * 512],
                                    slu[:, dt * 128:(dt + 1) * 128],
                                    AR1[:, dt * 1024 + tc * 512: dt * 1024 + tc * 512 + 512],
                                    start=(dt == 0), stop=(dt == ND - 1))
            gu_mm[ju] = inc(last, "te")
        block.tensor(ft)

        def fs(s):
            wt(s, "te", gu_mm[ju])
            if k == 0:
                wt(s, "ve", cast_done[ND - 1])
                wt(s, "te", stats_mm_done)
            if k >= 2:
                wt(s, "ve", fold_done[k - 2])
            s.activation(out=AR4[:, sgo:sgo + 1024], in_=gQ[:, :], func=AF.Silu)
            guev[k] = inc(
                s.activation(out=AR4[:, uo:uo + 1024], in_=uQ[:, :],
                             func=AF.Copy), "se")
        block.scalar(fs)

        def fv(v):
            wt(v, "se", guev[k])
            if e < 8 and ft_ == 0:
                wt(v, "ve", gbc_done[e])
            if e >= 1 and ft_ == 0:
                wt(v, "te", dn_mm[(e - 1) * ND + ND - 1])
            asl = AR4[:, ft_ * 1024:(ft_ + 1) * 1024]
            vbar(v, v.tensor_mul(asl, AR4[:, sgo:sgo + 1024], AR4[:, uo:uo + 1024]))
            if e < 8:
                fold_done[k] = vbar(v, v.tensor_tensor(asl, asl, gbc_sb[:, :],
                                                       op=ALU.mult))
            else:
                fold_done[k] = cnt["ve"]
        block.vector(fv)

    def emit_down(e, dt):
        j = e * ND + dt

        dQ = Q2 if dt % 2 == 0 else Q3

        def ft(t):
            wt(t, wd_ld[j][0], wd_ld[j][1])
            wt(t, "ve", fold_done[e * NF + NF - 1])
            if j >= 2:
                wt(t, "ve", dn_add[j - 2])
            sl = wd_r[:, (j % 4) * 1024:(j % 4) * 1024 + 1024]
            for tc in range(2):
                for ft_ in range(NF):
                    last = t.matmul(dQ[:, tc * 512:(tc + 1) * 512],
                                    sl[:, ft_ * 128:(ft_ + 1) * 128],
                                    AR4[:, ft_ * 1024 + tc * 512: ft_ * 1024 + tc * 512 + 512],
                                    start=(ft_ == 0), stop=(ft_ == NF - 1))
            dn_mm[j] = inc(last, "te")
        block.tensor(ft)

        def fv(v):
            wt(v, "te", dn_mm[j])
            v.tensor_add(x1T(dt, 0, 512), dQ[:, 0:512], x1T(dt, 0, 512))
            dn_add[j] = inc(
                v.tensor_add(x1T(dt, 512, 512), dQ[:, 512:1024], x1T(dt, 512, 512)), "ve")
        block.vector(fv)

    emit_gu_load(0)
    emit_gu_load(1)
    emit_gu_load(2)
    next_gu_ld = 3
    next_wd = [0]

    def emit_wd_next():
        j = next_wd[0]
        if j < NE * ND:
            emit_wd_load(j // ND, j % ND)
            next_wd[0] += 1

    for _ in range(4):
        emit_wd_next()
    for e in range(NE):
        if e < 8:
            emit_bcg(e)
        for ft_ in range(NF):
            emit_gu(e, ft_)
            for _ in range(2):
                if next_gu_ld < ngu:
                    emit_gu_load(next_gu_ld)
                    next_gu_ld += 1
        for dt in range(ND):
            emit_down(e, dt)
            emit_wd_next()
    ffn_done = dn_add[NE * ND - 1]
    last_guev = guev[NE * NF - 1]

    if upto <= 10:
        def f_final(sy):
            wt(sy, "qs", cnt["qs"])
        block.sync(f_final)
        blk_es.close()
        es.close()
        return nc, dbg_out

    p7_tr = [0] * NT
    p7_ev = [0] * NT
    p7_st = [0] * NT

    for i in range(NT):
        pA = Q0 if i % 2 == 0 else Q2
        pB = Q1 if i % 2 == 0 else Q3
        def ft(t, i=i, pA=pA, pB=pB):
            wt(t, "ve", ffn_done)
            if i == 0:
                wt(t, "se", last_guev)
            if i >= 2:
                wt(t, "ve", p7_ev[i - 2])
            for dt in range(ND):
                tgt = pA if dt < 8 else pB
                last = t.transpose(tgt[:, (dt % 8) * 128:((dt % 8) + 1) * 128],
                                   x1T(dt, i * 128, 128), ident_f[:, :])
            p7_tr[i] = inc(last, "te")
        block.tensor(ft)

        def fv(v, i=i, pA=pA, pB=pB):
            wt(v, "te", p7_tr[i])
            if i >= 2:
                wt(v, "qs", p7_st[i - 2])
            if i == 0:
                wt(v, "ve", fold_done[NE * NF - 1])
            osl = AR4f[:, (i % 2) * 2048:(i % 2) * 2048 + 2048]
            v.tensor_copy(osl[:, 0:1024], pA[:, :])
            p7_ev[i] = inc(v.tensor_copy(osl[:, 1024:2048], pB[:, :]), "ve")
        block.vector(fv)

        def fy(sy, i=i):
            wt(sy, "ve", p7_ev[i])
            osl = AR4f[:, (i % 2) * 2048:(i % 2) * 2048 + 2048]
            p7_st[i] = inc(
                sy.dma_start(out=o_my[i * 128:(i + 1) * 128, :], in_=osl), "qs", 16)
        block.sync(fy)

    def f_final(sy):
        wt(sy, "qs", cnt["qs"])
    block.sync(f_final)

    blk_es.close()
    es.close()
    return nc, dbg_out


def _prep_weights(L0_attn_w_qkv, L0_attn_w_o, L0_ffn_rms_weight, L0_ffn_router_w,
                  L0_ffn_w_gate, L0_ffn_w_up, L0_ffn_w_down,
                  L0_ffn_sw_gate, L0_ffn_sw_up, L0_ffn_sw_down):
    f = np.float32
    wqkv = np.ascontiguousarray(
        np.asarray(L0_attn_w_qkv, f).reshape(16, 128, 48, 128).transpose(2, 1, 0, 3))
    wo = np.ascontiguousarray(
        np.asarray(L0_attn_w_o, f).reshape(16, 128, 16, 128).transpose(2, 1, 0, 3))
    router = np.ascontiguousarray(
        np.asarray(L0_ffn_router_w, f).reshape(16, 128, 8).transpose(1, 0, 2))
    rmst = np.ascontiguousarray(np.asarray(L0_ffn_rms_weight, f).reshape(16, 128).T)
    rmsrow = np.ascontiguousarray(np.asarray(L0_ffn_rms_weight, f).reshape(1, 2048))
    wg8 = np.asarray(L0_ffn_w_gate, f).reshape(8, 16, 128, 8, 128).transpose(0, 3, 2, 1, 4)
    swg = np.asarray(L0_ffn_sw_gate, f).reshape(16, 128, 8, 128).transpose(2, 1, 0, 3)
    wg = np.ascontiguousarray(np.concatenate([wg8, swg[None]], axis=0))
    wu8 = np.asarray(L0_ffn_w_up, f).reshape(8, 16, 128, 8, 128).transpose(0, 3, 2, 1, 4)
    swu = np.asarray(L0_ffn_sw_up, f).reshape(16, 128, 8, 128).transpose(2, 1, 0, 3)
    wu = np.ascontiguousarray(np.concatenate([wu8, swu[None]], axis=0))
    wd8 = np.asarray(L0_ffn_w_down, f).reshape(8, 8, 128, 16, 128).transpose(0, 3, 2, 1, 4)
    swd = np.asarray(L0_ffn_sw_down, f).reshape(8, 128, 16, 128).transpose(2, 1, 0, 3)
    wd = np.ascontiguousarray(np.concatenate([wd8, swd[None]], axis=0))
    return dict(wqkv=wqkv, wo=wo, router=router, rmst_in=rmst, rmsrow_in=rmsrow,
                wg_in=wg, wu_in=wu, wd_in=wd)


def make_in_maps(inputs):
    x2d = np.ascontiguousarray(np.asarray(inputs["L0_attn_x"], np.float32).reshape(8192, 2048))
    wmap = _prep_weights(
        inputs["L0_attn_w_qkv"], inputs["L0_attn_w_o"], inputs["L0_ffn_rms_weight"],
        inputs["L0_ffn_router_w"], inputs["L0_ffn_w_gate"], inputs["L0_ffn_w_up"],
        inputs["L0_ffn_w_down"], inputs["L0_ffn_sw_gate"], inputs["L0_ffn_sw_up"],
        inputs["L0_ffn_sw_down"])
    in_maps = []
    for c in range(NCORE):
        m = dict(wmap)
        m["x_my"] = np.ascontiguousarray(x2d[c * TPC:(c + 1) * TPC])
        m["hsel"] = np.array([[1.0, 0.0]] if c % 2 == 0 else [[0.0, 1.0]], np.float32)
        in_maps.append(m)
    return in_maps


def kernel(**inputs):
    in_maps = make_in_maps(inputs)
    nc, _ = build_program()
    res = run_bass_kernel_spmd(nc, in_maps, core_ids=list(range(NCORE)))
    out = np.concatenate([res.results[c]["o_my"] for c in range(NCORE)], axis=0)
    return np.ascontiguousarray(out.reshape(4, 2048, 2048))
